# version 24
# baseline (speedup 1.0000x reference)
import numpy as np
from contextlib import ExitStack

import concourse.bass as bass
import concourse.mybir as mybir
from concourse.bass_utils import run_bass_kernel_spmd
from concourse.aot_env import allow_ici_all_to_all

F32 = mybir.dt.float32
AF = mybir.ActivationFunctionType
ALU = mybir.AluOpType
AX = mybir.AxisListType

NCORE = 8
S = 2048
D = 1024
H = 8
TC = 16
TLOC = 256
SL = 8
RMS_EPS = float(np.float32(1.1920929e-07))
LN_EPS = 1e-5
PI_HALF = float(np.pi / 2)


class Ch:
    def __init__(self, nc, es, name):
        self.sem = es.enter_context(nc.semaphore(name))
        self.n = 0


class Prog:
    ENGS = ("sync", "tensor", "vector", "scalar", "gpsimd")

    def __init__(self, nc):
        self.nc = nc
        self.ops = {e: [] for e in self.ENGS}

    def op(self, eng, fn, wait=(), inc=()):
        rw = [(ch, ch.n if v is None else v) for ch, v in wait]
        ri = []
        for ch, n in inc:
            ch.n += n
            ri.append((ch, n))
        self.ops[eng].append((fn, rw, ri))

    def fence(self, chans):
        vals = [(ch, ch.n) for ch in chans]
        for e in self.ENGS:
            self.ops[e].append((None, list(vals), []))

    def emit(self):
        with self.nc.Block() as block:
            for eng in self.ENGS:
                ops = self.ops[eng]
                if not ops:
                    continue

                def body(e, ops=ops):
                    lastw = {}
                    for fn, rw, ri in ops:
                        for ch, v in rw:
                            if v > 0 and lastw.get(id(ch), -1) < v:
                                e.wait_ge(ch.sem, v)
                                lastw[id(ch)] = v
                        if fn is None:
                            continue
                        instr = fn(e)
                        for ch, n in ri:
                            instr.then_inc(ch.sem, n)

                getattr(block, eng)(body)


def apx(handle, off, dims):
    w = int(handle.shape[1])
    return bass.AP(handle, off, [[w, 128]] + [list(d) for d in dims])


def build_program():
    nc = bass.Bass("TRN2", target_bir_lowering=False, debug=False,
                   detect_race_conditions=False)

    def din(name, shape):
        return nc.dram_tensor(name, shape, F32, kind="ExternalInput").ap()

    def dscr(name, shape):
        return nc.dram_tensor(name, shape, F32).ap()

    xb = din("xb", [S, D])
    wqT = din("wqT", [D, D])
    wkT = din("wkT", [D, D])
    wvT = din("wvT", [D, D])
    woT = din("woT", [D, D])
    nw_col = din("nw_col", [128, 8])
    bo = din("bo", [1, D])
    dw_in = din("dw_in", [1, 1])
    wg = din("wg", [D, 24])
    r2 = din("r2", [128, D])
    winT = din("winT", [D, 3 * D])
    b_in = din("b_in", [1, 3 * D])
    kde_raw = din("kde_raw", [11, 64])
    vde_raw = din("vde_raw", [11, 64])
    woutT = din("woutT", [D, D])
    b_out = din("b_out", [1, D])
    theta = din("theta", [D, D])
    thn = din("thn", [D, D])
    phi = din("phi", [D, D])
    phn = din("phn", [D, D])
    ident_in = din("ident", [128, 128])
    out_ext = nc.dram_tensor("out", [SL, TLOC, D], F32, kind="ExternalOutput").ap()

    qscr = dscr("qscr", [D, S])
    kscr = dscr("kscr", [D, S])
    vscr = dscr("vscr", [H, S, 128])
    ccin = dscr("ccin", [S, D])
    ccout = dscr("ccout", [SL, TLOC, D])
    cwscr = dscr("cwscr", [D, D])
    swscr = dscr("swscr", [D, D])

    es = ExitStack()
    sb = lambda name, shape: es.enter_context(nc.sbuf_tensor(name, shape, F32))

    ld = Ch(nc, es, "ld")
    g = Ch(nc, es, "g")
    gc = Ch(nc, es, "gc")
    cc = Ch(nc, es, "cc")
    mm = Ch(nc, es, "mm")
    ve = Ch(nc, es, "ve")
    sc = Ch(nc, es, "sc")
    CHANS = [ld, g, gc, cc, mm, ve, sc]

    p = Prog(nc)

    pS = es.enter_context(nc.psum_tensor("pS", [128, 2048], F32))
    pT = es.enter_context(nc.psum_tensor("pT", [128, 2048], F32))

    class BankView:
        def __init__(self, parent, off):
            self.parent, self.off = parent, off

        def __getitem__(self, idx):
            if idx == slice(None):
                return self.parent[:, self.off:self.off + 512]
            ps, fs = idx
            a = fs.start or 0
            b = fs.stop if fs.stop is not None else 512
            return self.parent[ps, self.off + a:self.off + b]

    BANKS = [BankView(pS, i * 512) for i in range(4)] +             [BankView(pT, i * 512) for i in range(4)]
    pb0, pb1, pb2, pb3, pb4, pb5, pb6, pb7 = BANKS

    ident = sb("ident_sb", [128, 128])
    ones_row = sb("ones_row", [33, 512])
    bo_sb = sb("bo_sb", [1, D])
    b_in_sb = sb("b_in_sb", [33, 128])
    b_out_sb = sb("b_out_sb", [1, D])
    nw_sb = sb("nw_sb", [128, 8])
    dw_sb = sb("dw_sb", [1, 1])
    dwb = sb("dwb", [128, 1])
    ndw = sb("ndw", [128, 1])
    omdw = sb("omdw", [128, 1])
    kdeT = sb("kdeT", [128, 8])
    gates_buf = sb("gates_buf", [128, TC * 24])
    ln_t = [sb(f"ln{i}", [128, 1]) for i in range(10)]
    eps_ln = sb("eps_ln", [128, 1])
    eps_rms = sb("eps_rms", [128, 1])
    pih_sb = sb("pih_sb", [128, 1])

    p.op("sync", lambda e: e.dma_start(out=ident[:], in_=ident_in), inc=[(ld, 16)])
    p.op("sync", lambda e: e.dma_start(out=bo_sb[:], in_=bo), inc=[(ld, 16)])
    p.op("sync", lambda e: e.dma_start(out=b_out_sb[:], in_=b_out), inc=[(ld, 16)])
    p.op("sync", lambda e: e.dma_start(out=nw_sb[:], in_=nw_col), inc=[(ld, 16)])
    p.op("sync", lambda e: e.dma_start(out=dw_sb[:], in_=dw_in), inc=[(ld, 16)])
    kde8 = sb("kde8", [8, 64])
    p.op("sync", lambda e: e.dma_start(out=kde8[:], in_=kde_raw[0:8, :]), inc=[(ld, 16)])
    def _consts(e):
        e.memset(ones_row[:], 1.0)
        e.memset(eps_ln[:], LN_EPS)
        e.memset(eps_rms[:], RMS_EPS)
        return e.memset(pih_sb[:], PI_HALF)
    p.op("vector", _consts, inc=[(ve, 1)])
    p.op("tensor", lambda e: e.matmul(pb0[:, 0:1], ones_row[0:1, 0:128], dw_sb[:],
                                      start=True, stop=True),
         wait=[(ld, None), (ve, None)], inc=[(mm, 1)])

    def _dw(e):
        e.tensor_copy(dwb[:], pb0[:, 0:1])
        e.drain()
        e.tensor_scalar(ndw[:], dwb[:], -1.0, None, ALU.mult)
        e.drain()
        return e.tensor_scalar(omdw[:], dwb[:], -1.0, 1.0, ALU.mult, ALU.add)
    p.op("vector", _dw, wait=[(mm, None)], inc=[(ve, 1)])
    p.op("tensor", lambda e: e.transpose(pb1[0:64, 0:8], kde8[:], ident[0:8, 0:8]),
         wait=[(ld, None)], inc=[(mm, 1)])
    p.op("scalar", lambda e: e.activation(kdeT[0:64, :], pb1[0:64, 0:8], AF.Copy),
         wait=[(mm, None)], inc=[(sc, 1)])
    p.op("gpsimd", lambda e: e.dma_start(out=kdeT[64:128, :], in_=kdeT[0:64, :]),
         wait=[(sc, None)], inc=[(g, 16)])

    def emit_ln(x_ap, xn_ap, width, sqbuf):
        inv = 1.0 / width

        def v1(e):
            e.reduce_sum(ln_t[0][:], x_ap, axis=AX.X)
            e.drain()
            return e.tensor_scalar(ln_t[1][:], ln_t[0][:], inv, None, ALU.mult)
        p.op("vector", v1, wait=[(ld, None), (cc, None)], inc=[(ve, 1)])
        p.op("scalar", lambda e: e.activation(sqbuf, x_ap, AF.Square),
             wait=[(ld, None), (ve, None), (cc, None)], inc=[(sc, 1)])

        def v2(e):
            e.reduce_sum(ln_t[2][:], sqbuf, axis=AX.X)
            e.drain()
            e.tensor_scalar(ln_t[3][:], ln_t[2][:], inv, None, ALU.mult)
            e.tensor_mul(ln_t[4][:], ln_t[1][:], ln_t[1][:])
            e.drain()
            return e.tensor_sub(ln_t[5][:], ln_t[3][:], ln_t[4][:])
        p.op("vector", v2, wait=[(sc, None)], inc=[(ve, 1)])
        p.op("scalar", lambda e: e.activation(ln_t[6][:], ln_t[5][:], AF.Sqrt,
                                              bias=eps_ln[:], scale=1.0),
             wait=[(ve, None)], inc=[(sc, 1)])

        def v3(e):
            e.reciprocal(ln_t[7][:], ln_t[6][:])
            e.drain()
            e.tensor_scalar(ln_t[8][:], ln_t[1][:], -1.0, None, ALU.mult)
            e.drain()
            return e.tensor_scalar(xn_ap, x_ap, ln_t[8][:], ln_t[7][:],
                                   ALU.add, ALU.mult)
        p.op("vector", v3, wait=[(sc, None), (mm, None)], inc=[(ve, 1)])
        p.op("scalar", lambda e: e.activation(ln_t[0][:], ln_t[7][:], AF.Copy),
             wait=[(ve, None)], inc=[(sc, 1)])

    def a_pass1():
        ar = ExitStack()
        asb = lambda n, s: ar.enter_context(nc.sbuf_tensor(n, s, F32))
        wq_sb = asb("wq_sb", [128, 8 * D])
        wk_sb = asb("wk_sb", [128, 8 * D])
        wv_sb = asb("wv_sb", [128, 8 * D])
        wg_sb = asb("wg_sb", [128, 8 * 24])
        x_tc = asb("x_tc", [128, D])
        sq_tc = asb("sq_tc", [128, D])
        xn_tc = asb("xn_tc", [128, D])
        xnf = asb("xnf", [128, 8 * 512])
        qkbuf = asb("qkbuf", [128, 1024])
        vbuf = asb("vbuf", [128, D])

        for j in range(8):
            for wsb, src_ in ((wq_sb, wqT), (wk_sb, wkT), (wv_sb, wvT)):
                p.op("sync", lambda e, j=j, wsb=wsb, src_=src_: e.dma_start(
                    out=wsb[:, j * D:(j + 1) * D],
                    in_=src_[j * 128:(j + 1) * 128, :]), inc=[(ld, 16)])
            p.op("sync", lambda e, j=j: e.dma_start(
                out=wg_sb[:, j * 24:(j + 1) * 24],
                in_=wg[j * 128:(j + 1) * 128, :]), inc=[(ld, 16)])

        for sci in range(4):
            t0s = sci * 512
            for sub in range(4):
                t0 = t0s + sub * 128
                p.op("sync", lambda e, t0=t0: e.dma_start(out=x_tc[:], in_=xb[t0:t0 + 128, :]),
                     wait=[(sc, None)], inc=[(ld, 16)])
                emit_ln(x_tc[:], xn_tc[:], D, sq_tc[:])
                bank0 = BANKS[4 + 2 * (sub % 2)]
                bank1 = BANKS[5 + 2 * (sub % 2)]

                def tr(e, bank0=bank0, bank1=bank1):
                    last = None
                    for jj in range(8):
                        bank = bank0 if jj < 4 else bank1
                        last = e.transpose(bank[:, (jj % 4) * 128:(jj % 4) * 128 + 128],
                                           xn_tc[:, jj * 128:(jj + 1) * 128], ident[:])
                    return last
                p.op("tensor", tr, wait=[(sc, None)], inc=[(mm, 1)])

                def trev(e, sub=sub, bank0=bank0, bank1=bank1):
                    o0 = bass.AP(xnf, sub * 128, [[4096, 128], [512, 4], [1, 128]])
                    e.activation(o0, bank0[:], AF.Copy)
                    o1_ = bass.AP(xnf, 4 * 512 + sub * 128, [[4096, 128], [512, 4], [1, 128]])
                    return e.activation(o1_, bank1[:], AF.Copy)
                p.op("scalar", trev, wait=[(mm, None)], inc=[(sc, 1)])

            qk_slot_ev = getattr(a_pass1, "_slot_ev", {0: 0, 512: 0})
            for wsb, scr in ((wq_sb, qscr), (wk_sb, kscr)):
                for ic in range(8):
                    bank = BANKS[ic % 4]
                    slot = (ic % 2) * 512

                    def mm1(e, wsb=wsb, ic=ic, bank=bank):
                        last = None
                        for j in range(8):
                            last = e.matmul(
                                bank[:],
                                wsb[:, j * D + ic * 128: j * D + (ic + 1) * 128],
                                xnf[:, j * 512:(j + 1) * 512],
                                start=(j == 0), stop=(j == 7))
                        return last
                    p.op("tensor", mm1, wait=[(sc, None)], inc=[(mm, 1)])
                    p.op("scalar", lambda e, bank=bank, slot=slot: e.activation(
                        qkbuf[:, slot:slot + 512], bank[:], AF.Copy),
                        wait=[(mm, None), (g, qk_slot_ev[slot])], inc=[(sc, 1)])
                    p.op("gpsimd", lambda e, ic=ic, scr=scr, t0s=t0s, slot=slot: e.dma_start(
                        out=scr[ic * 128:(ic + 1) * 128, t0s:t0s + 512],
                        in_=qkbuf[:, slot:slot + 512]),
                        wait=[(sc, None)], inc=[(g, 16)])
                    qk_slot_ev[slot] = g.n
            a_pass1._slot_ev = qk_slot_ev

            for sub in range(4):
                for vh in range(2):
                    bank = BANKS[4 + vh]

                    def mmv(e, sub=sub, vh=vh, bank=bank):
                        last = None
                        for j in range(8):
                            last = e.matmul(
                                bank[:],
                                xnf[:, j * 512 + sub * 128: j * 512 + sub * 128 + 128],
                                wv_sb[:, j * D + vh * 512: j * D + (vh + 1) * 512],
                                start=(j == 0), stop=(j == 7))
                        return last
                    p.op("tensor", mmv, wait=[(sc, None)], inc=[(mm, 1)])
                    p.op("scalar", lambda e, vh=vh, bank=bank: e.activation(
                        vbuf[:, vh * 512:(vh + 1) * 512], bank[:], AF.Copy),
                        wait=[(mm, None), (g, None)], inc=[(sc, 1)])
                for h in range(H):
                    p.op("gpsimd", lambda e, h=h, t0s=t0s, sub=sub: e.dma_start(
                        out=vscr[h, t0s + sub * 128:t0s + (sub + 1) * 128, :],
                        in_=vbuf[:, h * 128:(h + 1) * 128]),
                        wait=[(sc, None)], inc=[(g, 16)])

                def mmg(e, sub=sub):
                    last = None
                    for j in range(8):
                        last = e.matmul(pb6[:, 448:472],
                                        xnf[:, j * 512 + sub * 128: j * 512 + sub * 128 + 128],
                                        wg_sb[:, j * 24:(j + 1) * 24],
                                        start=(j == 0), stop=(j == 7))
                    return last
                p.op("tensor", mmg, wait=[(sc, None)], inc=[(mm, 1)])
                p.op("scalar", lambda e, sub=sub, t0s=t0s: e.activation(
                    gates_buf[:, (t0s // 128 + sub) * 24:(t0s // 128 + sub + 1) * 24],
                    pb6[:, 448:472], AF.Copy),
                    wait=[(mm, None)], inc=[(sc, 1)])

        p.fence(CHANS)
        ar.close()

    def a_attn(o_full):
        ar = ExitStack()
        asb = lambda n, s: ar.enter_context(nc.sbuf_tensor(n, s, F32))
        qp0b = [asb(f"qp0{i}", [128, S]) for i in range(2)]
        qp1b = [asb(f"qp1{i}", [128, S]) for i in range(2)]
        khb = [asb(f"kh{i}", [128, S]) for i in range(2)]
        def _zpad(e):
            e.memset(qp0b[0][64:128, :], 0.0)
            e.memset(qp0b[1][64:128, :], 0.0)
            e.memset(qp1b[0][0:64, :], 0.0)
            return e.memset(qp1b[1][0:64, :], 0.0)
        p.op("vector", _zpad, inc=[(ve, 1)])
        vhb = [asb(f"vh{i}", [128, S]) for i in range(2)]
        e0 = asb("e0", [128, S])
        e1 = asb("e1", [128, S])
        att = asb("att", [128, S])
        attT = asb("attT", [128, S])
        d0 = asb("d0", [128, 1])
        d1 = asb("d1", [128, 1])
        d0a = asb("d0a", [128, 1])
        d0b = asb("d0b", [128, 1])
        d1a = asb("d1a", [128, 1])
        d1b = asb("d1b", [128, 1])
        a0s = [asb(f"a0v{i}", [128, 1]) for i in range(2)]
        c1 = asb("c1", [128, 1])
        m0 = asb("m0", [128, 1])
        m1 = asb("m1", [128, 1])
        t6 = asb("t6", [128, 1])
        t7 = asb("t7", [128, 1])

        def rec_loads(h, wait_mm):
            par = h % 2
            p.op("sync", lambda e, h=h, par=par: e.dma_start(
                out=qp0b[par][0:64, :], in_=qscr[h * 128:h * 128 + 64, :]),
                wait=[(mm, wait_mm)], inc=[(ld, 16)])
            p.op("sync", lambda e, h=h, par=par: e.dma_start(
                out=qp1b[par][64:128, :], in_=qscr[h * 128 + 64:(h + 1) * 128, :]),
                inc=[(ld, 16)])
            p.op("sync", lambda e, h=h, par=par: e.dma_start(
                out=khb[par][:], in_=kscr[h * 128:(h + 1) * 128, :]), inc=[(ld, 16)])
            for kc in range(16):
                p.op("sync", lambda e, h=h, kc=kc, par=par: e.dma_start(
                    out=vhb[par][:, kc * 128:(kc + 1) * 128],
                    in_=vscr[h, kc * 128:(kc + 1) * 128, :]), inc=[(ld, 16)])
            return ld.n

        head_done_mm = {-1: mm.n}
        ld_head = {0: rec_loads(0, mm.n)}

        for h in range(H):
            par = h % 2
            qp0, qp1, kh, vh = qp0b[par], qp1b[par], khb[par], vhb[par]
            base_sc, base_ve, base_mm = sc.n, ve.n, mm.n
            ev = {}
            pv = {}

            def emit_tail1(pq, pv):
                def tr_a(e):
                    last = None
                    for kc in range(8):
                        last = e.transpose(pT[:, kc * 128:(kc + 1) * 128],
                                           att[:, kc * 128:(kc + 1) * 128], ident[:])
                    return last
                p.op("tensor", tr_a, wait=[(ve, pv["comb_a"]),
                                           (sc, pv.get("attev_m2", base_sc)),
                                           (ve, pv.get("oevac_m2", base_ve))], inc=[(mm, 1)])
                tr_a_ev = mm.n

                def tr_b(e):
                    last = None
                    for kc in range(8, 16):
                        last = e.transpose(pT[:, kc * 128:(kc + 1) * 128],
                                           att[:, kc * 128:(kc + 1) * 128], ident[:])
                    return last
                p.op("tensor", tr_b, wait=[(ve, pv["combine"])], inc=[(mm, 1)])
                pv["tr"] = mm.n
                p.op("scalar", lambda e: e.activation(attT[:, 0:1024], pT[:, 0:1024],
                                                      AF.Copy),
                     wait=[(mm, max(tr_a_ev, pv.get("av_m1", base_mm)))], inc=[(sc, 1)])
                pv["attev_a"] = sc.n
                p.op("scalar", lambda e: e.activation(attT[:, 1024:2048],
                                                      pT[:, 1024:2048], AF.Copy),
                     wait=[(mm, pv["tr"])], inc=[(sc, 1)])
                pv["attev"] = sc.n

            def emit_tail2(pq, pv):
                def av_a(e, vh=vh):
                    last = None
                    for kc in range(8):
                        last = e.matmul(pT[:, 0:128],
                                        attT[:, kc * 128:(kc + 1) * 128],
                                        vh[:, kc * 128:(kc + 1) * 128],
                                        start=(kc == 0), stop=False)
                    return last
                p.op("tensor", av_a, wait=[(sc, pv["attev_a"])], inc=[(mm, 1)])

                def av_b(e, vh=vh):
                    last = None
                    for kc in range(8, 16):
                        last = e.matmul(pT[:, 0:128],
                                        attT[:, kc * 128:(kc + 1) * 128],
                                        vh[:, kc * 128:(kc + 1) * 128],
                                        start=False, stop=(kc == 15))
                    return last
                p.op("tensor", av_b, wait=[(sc, pv["attev"])], inc=[(mm, 1)])
                pv["av"] = mm.n
                p.op("vector", lambda e, h=h, pq=pq: e.tensor_scalar(
                    o_full[:, pq * D + h * 128: pq * D + (h + 1) * 128],
                    pT[:, 0:128], a0s[pq % 2][:], None, ALU.mult),
                    wait=[(mm, pv["av"])], inc=[(ve, 1)])
                pv["oevac"] = ve.n

            for qc in range(16):
                q0 = qc * 128
                def s0a(e, q0=q0, qp0=qp0, kh=kh):
                    last = None
                    for kb in range(2):
                        last = e.matmul(pS[:, kb * 512:(kb + 1) * 512],
                                        qp0[:, q0:q0 + 128],
                                        kh[:, kb * 512:(kb + 1) * 512],
                                        start=True, stop=True)
                    return last
                p.op("tensor", s0a,
                     wait=[(ld, ld_head[h]), (sc, pv.get("exp1a", base_sc))], inc=[(mm, 1)])

                def s0b(e, q0=q0, qp0=qp0, kh=kh):
                    last = None
                    for kb in range(2, 4):
                        last = e.matmul(pS[:, kb * 512:(kb + 1) * 512],
                                        qp0[:, q0:q0 + 128],
                                        kh[:, kb * 512:(kb + 1) * 512],
                                        start=True, stop=True)
                    return last
                p.op("tensor", s0b,
                     wait=[(sc, pv.get("exp1b", base_sc))], inc=[(mm, 1)])
                ev["s0"] = mm.n

                def max0(e):
                    e.reduce_max(t6[:], pS[:], axis=AX.X, negate=True)
                    e.drain()
                    return e.tensor_scalar(m0[:], t6[:], 0.125, None, ALU.mult)
                p.op("vector", max0, wait=[(mm, ev["s0"])], inc=[(ve, 1)])
                ev["max0"] = ve.n

                p.op("scalar", lambda e: e.activation(e0[:, 0:1024], pS[:, 0:1024],
                                                      AF.Exp, bias=m0[:], scale=0.125,
                                                      accum_out=d0a[:]),
                     wait=[(ve, ev["max0"])], inc=[(sc, 1)])
                ev["exp0a"] = sc.n
                p.op("scalar", lambda e: e.activation(e0[:, 1024:2048], pS[:, 1024:2048],
                                                      AF.Exp, bias=m0[:], scale=0.125,
                                                      accum_out=d0b[:]),
                     inc=[(sc, 1)])
                ev["exp0"] = sc.n

                if pv:
                    emit_tail1(qc - 1, pv)

                def s1a(e, q0=q0, qp1=qp1, kh=kh):
                    last = None
                    for kb in range(2):
                        last = e.matmul(pS[:, kb * 512:(kb + 1) * 512],
                                        qp1[:, q0:q0 + 128],
                                        kh[:, kb * 512:(kb + 1) * 512],
                                        start=True, stop=True)
                    return last
                p.op("tensor", s1a, wait=[(sc, ev["exp0a"])], inc=[(mm, 1)])

                def s1b(e, q0=q0, qp1=qp1, kh=kh):
                    last = None
                    for kb in range(2, 4):
                        last = e.matmul(pS[:, kb * 512:(kb + 1) * 512],
                                        qp1[:, q0:q0 + 128],
                                        kh[:, kb * 512:(kb + 1) * 512],
                                        start=True, stop=True)
                    return last
                p.op("tensor", s1b, wait=[(sc, ev["exp0"])], inc=[(mm, 1)])
                ev["s1"] = mm.n

                def max1(e):
                    e.reduce_max(t6[:], pS[:], axis=AX.X, negate=True)
                    e.drain()
                    return e.tensor_scalar(m1[:], t6[:], 0.125, None, ALU.mult)
                p.op("vector", max1, wait=[(mm, ev["s1"])], inc=[(ve, 1)])
                ev["max1"] = ve.n

                p.op("scalar", lambda e: e.activation(e1[:, 0:1024], pS[:, 0:1024],
                                                      AF.Exp, bias=m1[:], scale=0.125,
                                                      accum_out=d1a[:]),
                     wait=[(ve, ev["max1"])], inc=[(sc, 1)])
                ev["exp1a"] = sc.n
                p.op("scalar", lambda e: e.activation(e1[:, 1024:2048], pS[:, 1024:2048],
                                                      AF.Exp, bias=m1[:], scale=0.125,
                                                      accum_out=d1b[:]),
                     inc=[(sc, 1)])
                ev["exp1b"] = sc.n
                ev["exp1"] = sc.n

                if pv:
                    emit_tail2(qc - 1, pv)

                def comb_a(e, qc=qc):
                    e.tensor_add(d0[:], d0a[:], d0b[:])
                    e.tensor_add(d1[:], d1a[:], d1b[:])
                    e.drain()
                    e.reciprocal(a0s[qc % 2][:], d0[:])
                    e.reciprocal(t6[:], d1[:])
                    e.drain()
                    e.tensor_mul(t7[:], d0[:], t6[:])
                    e.drain()
                    e.tensor_mul(c1[:], t7[:], ndw[:])
                    e.drain()
                    return e.scalar_tensor_tensor(att[:, 0:1024], e1[:, 0:1024],
                                                  c1[:], e0[:, 0:1024],
                                                  ALU.mult, ALU.add)
                p.op("vector", comb_a,
                     wait=[(sc, ev["exp1"])] + ([(mm, pv["tr"])] if pv else []),
                     inc=[(ve, 1)])
                ev["comb_a"] = ve.n
                p.op("vector", lambda e: e.scalar_tensor_tensor(
                    att[:, 1024:2048], e1[:, 1024:2048], c1[:], e0[:, 1024:2048],
                    ALU.mult, ALU.add), inc=[(ve, 1)])
                ev["comb_g"] = gc.n
                ev["combine"] = ve.n

                ev["attev_m2"] = pv.get("attev", 0)
                ev["oevac_m2"] = pv.get("oevac", 0)
                ev["av_m1"] = pv.get("av", 0)
                pv = ev
                ev = {}

                if qc == 2 and h + 1 < H:
                    ld_head[h + 1] = rec_loads(h + 1, head_done_mm[h - 1])

            emit_tail1(15, pv)
            emit_tail2(15, pv)
            head_done_mm[h] = mm.n
        p.fence(CHANS)
        ar.close()

    def a_pass3(o_full):
        ar = ExitStack()
        asb = lambda n, s: ar.enter_context(nc.sbuf_tensor(n, s, F32))
        wo_sb = asb("wo_sb", [128, 8 * D])
        r2_sb = asb("r2_sb", [128, D])
        x_tc = asb("x_tc3", [128, D])
        sq_tc = asb("sq_tc3", [128, D])
        g0_sb = asb("g0_sb", [128, D])
        x1_sb = asb("x1_sb", [128, D])
        srt8 = asb("srt8", [128, 8])
        gm = [asb(f"gm{i}", [128, 1]) for i in range(4)]
        mask8 = asb("mask8", [128, 8])
        sge = asb("sge", [128, 8])
        sgu = asb("sgu", [128, 8])
        gate8 = asb("gate8", [128, 8])
        z_sb = asb("z_sb", [128, 128])
        zT_sb = asb("zT_sb", [128, 128])
        trig_a = asb("trig_a", [128, D])
        trig_b = asb("trig_b", [128, D])
        trig_c = asb("trig_c", [128, D])

        def emit_trig(idx):
            kind, rb = idx // 8, idx % 8
            src0 = theta if kind == 0 else phi
            src1 = thn if kind == 0 else phn
            dst = cwscr if kind == 0 else swscr
            r0 = rb * 128
            p.op("sync", lambda e, src0=src0, r0=r0: e.dma_start(
                out=trig_a[:], in_=src0[r0:r0 + 128, :]),
                wait=[(sc, None)], inc=[(ld, 16)])
            p.op("sync", lambda e, src1=src1, r0=r0: e.dma_start(
                out=trig_b[:], in_=src1[r0:r0 + 128, :]), inc=[(ld, 16)])
            p.op("vector", lambda e: e.tensor_add(trig_a[:], trig_a[:], trig_b[:]),
                 wait=[(ld, None), (sc, None)], inc=[(ve, 1)])
            if kind == 0:
                p.op("scalar", lambda e: e.activation(
                    trig_c[:], trig_a[:], AF.Sin, bias=pih_sb[:], scale=1.0),
                    wait=[(ve, None), (g, None)], inc=[(sc, 1)])
            else:
                p.op("scalar", lambda e: e.activation(
                    trig_c[:], trig_a[:], AF.Sin, bias=0.0, scale=1.0),
                    wait=[(ve, None), (g, None)], inc=[(sc, 1)])
            p.op("gpsimd", lambda e, dst=dst, r0=r0: e.dma_start(
                out=dst[r0:r0 + 128, :], in_=trig_c[:]),
                wait=[(sc, None)], inc=[(g, 16)])

        for j in range(8):
            p.op("sync", lambda e, j=j: e.dma_start(
                out=wo_sb[:, j * D:(j + 1) * D],
                in_=woT[j * 128:(j + 1) * 128, :]), inc=[(ld, 16)])
        p.op("sync", lambda e: e.dma_start(out=r2_sb[:], in_=r2), inc=[(ld, 16)])
        for j in range(8):
            p.op("scalar", lambda e, j=j: e.activation(
                wo_sb[:, j * D:(j + 1) * D], wo_sb[:, j * D:(j + 1) * D], AF.Copy,
                bias=0.0, scale=nw_sb[:, j:j + 1]),
                wait=[(ld, None)], inc=[(sc, 1)])

        def stage_a(tc):
            oslab = o_full[:, tc * D:(tc + 1) * D]
            p.op("scalar", lambda e, oslab=oslab: e.activation(sq_tc[:], oslab, AF.Square),
                 wait=[(ve, None)], inc=[(sc, 1)])

            def rms_v(e):
                e.reduce_sum(gm[0][:], sq_tc[:], axis=AX.X)
                e.drain()
                return e.tensor_scalar(gm[1][:], gm[0][:], 1.0 / D, None, ALU.mult)
            p.op("vector", rms_v, wait=[(sc, None)], inc=[(ve, 1)])
            p.op("scalar", lambda e: e.activation(gm[2][:], gm[1][:], AF.Sqrt,
                                                  bias=eps_rms[:], scale=1.0),
                 wait=[(ve, None)], inc=[(sc, 1)])

            def rms_r(e, oslab=oslab):
                e.reciprocal(gm[3][:], gm[2][:])
                e.drain()
                return e.tensor_scalar(oslab, oslab, gm[3][:], None, ALU.mult)
            p.op("vector", rms_r, wait=[(sc, None)], inc=[(ve, 1)])

            bank0 = BANKS[2 * (tc % 2)]
            bank1 = BANKS[2 * (tc % 2) + 1]

            def otr(e, oslab=oslab, bank0=bank0, bank1=bank1):
                last = None
                for jj in range(8):
                    bank = bank0 if jj < 4 else bank1
                    last = e.transpose(bank[:, (jj % 4) * 128:(jj % 4) * 128 + 128],
                                       oslab[:, jj * 128:(jj + 1) * 128], ident[:])
                return last
            p.op("tensor", otr, wait=[(ve, None), (sc, None)], inc=[(mm, 1)])

            def otrev(e, oslab=oslab, bank0=bank0, bank1=bank1):
                e.activation(oslab[:, 0:512], bank0[:], AF.Copy)
                return e.activation(oslab[:, 512:1024], bank1[:], AF.Copy)
            p.op("scalar", otrev, wait=[(mm, None)], inc=[(sc, 1)])
            return sc.n

        aev = {}

        def stage_b(tc):
            oT = o_full[:, tc * D:(tc + 1) * D]
            for ih in range(2):
                bank = BANKS[4 + 2 * (tc % 2) + ih]

                def wo_mm(e, oT=oT, ih=ih, bank=bank):
                    for ec in range(8):
                        e.matmul(bank[:],
                                 oT[:, ec * 128:(ec + 1) * 128],
                                 wo_sb[:, ec * D + ih * 512: ec * D + (ih + 1) * 512],
                                 start=(ec == 0), stop=False)
                    return e.matmul(bank[:], ones_row[0:1, 0:128],
                                    bo_sb[0:1, ih * 512:(ih + 1) * 512],
                                    start=False, stop=True)
                p.op("tensor", wo_mm, wait=[(sc, aev[tc]), (ve, None)], inc=[(mm, 1)])
            for ih in range(2):
                bank = BANKS[4 + 2 * (tc % 2) + ih]
                p.op("vector", lambda e, oT=oT, ih=ih, bank=bank: e.tensor_scalar(
                    oT[:, ih * 512:(ih + 1) * 512], bank[:], omdw[:], None, ALU.mult),
                    wait=[(mm, None)], inc=[(ve, 1)])

        for tc in range(TC):
            aev[tc] = stage_a(tc)
            if tc > 0:
                stage_b(tc - 1)
        stage_b(TC - 1)

        s0_full = o_full
        for tc in range(TC):
            t0 = tc * 128
            gl = gates_buf[:, tc * 24:tc * 24 + 8]
            ce = gates_buf[:, tc * 24 + 8:tc * 24 + 24]

            def g_v1(e, gl=gl):
                e.max(srt8[:], gl)
                e.drain()
                e.tensor_scalar(gm[0][:], srt8[:, 0:1], -1.0, None, ALU.mult)
                e.drain()
                return e.tensor_single_scalar(mask8[:], gl, srt8[:, 1:2], ALU.is_ge)
            p.op("vector", g_v1, wait=[(sc, None)], inc=[(ve, 1)])
            p.op("scalar", lambda e, gl=gl: e.activation(sge[:], gl, AF.Exp,
                                                         bias=gm[0][:], scale=1.0),
                 wait=[(ve, None)], inc=[(sc, 1)])

            def g_v2(e, ce=ce):
                e.tensor_mul(sgu[:], sge[:], mask8[:])
                e.drain()
                e.reduce_sum(gm[2][:], sgu[:], axis=AX.X)
                e.drain()
                e.reciprocal(gm[3][:], gm[2][:])
                e.drain()
                e.tensor_scalar(gate8[:], sgu[:], gm[3][:], None, ALU.mult)
                e.drain()
                return e.tensor_tensor(
                    z_sb[:].rearrange("p (a b) -> p a b", a=8),
                    gate8[:].unsqueeze(2).to_broadcast([128, 8, 16]),
                    ce.unsqueeze(1).to_broadcast([128, 8, 16]),
                    ALU.mult)
            p.op("vector", g_v2, wait=[(sc, None)], inc=[(ve, 1)])
            p.op("tensor", lambda e: e.transpose(pb2[:, 0:128], z_sb[:], ident[:]),
                 wait=[(ve, None), (sc, None)], inc=[(mm, 1)])
            p.op("scalar", lambda e: e.activation(zT_sb[:], pb2[:, 0:128], AF.Copy),
                 wait=[(mm, None)], inc=[(sc, 1)])
            for oh in range(2):
                bank = BANKS[oh]
                p.op("tensor", lambda e, oh=oh, bank=bank: e.matmul(
                    bank[:], zT_sb[:], r2_sb[:, oh * 512:(oh + 1) * 512],
                    start=True, stop=True),
                    wait=[(sc, None)], inc=[(mm, 1)])
                p.op("scalar", lambda e, oh=oh, bank=bank: e.activation(
                    g0_sb[:, oh * 512:(oh + 1) * 512], bank[:], AF.Silu,
                    bias=0.0, scale=2.0),
                    wait=[(mm, None)], inc=[(sc, 1)])
            p.op("sync", lambda e, t0=t0: e.dma_start(out=x_tc[:], in_=xb[t0:t0 + 128, :]),
                 wait=[(ve, None)], inc=[(ld, 16)])

            def comb(e, tc=tc):
                e.tensor_mul(x1_sb[:], s0_full[:, tc * D:(tc + 1) * D], g0_sb[:])
                return e.tensor_add(x1_sb[:], x1_sb[:], x_tc[:])
            p.op("vector", comb, wait=[(sc, None), (ld, None), (g, None)], inc=[(ve, 1)])
            p.op("gpsimd", lambda e, t0=t0: e.dma_start(
                out=ccin[t0:t0 + 128, :], in_=x1_sb[:]),
                wait=[(ve, None)], inc=[(g, 16)])
            emit_trig(tc)
        p.fence(CHANS)
        ar.close()

    def a2a_and_trig():
        def coll(e):
            with allow_ici_all_to_all():
                return e.collective_compute(
                    "AllToAll", ALU.bypass,
                    replica_groups=[list(range(NCORE))],
                    ins=[ccin.opt()], outs=[ccout.opt()])
        p.op("gpsimd", coll, wait=[(g, None)], inc=[(cc, 1)])
        p.fence(CHANS)

    def phase_b():
        br = ExitStack()
        bsb = lambda n, s: br.enter_context(nc.sbuf_tensor(n, s, F32))
        y_tile = bsb("y_tile", [128, D])
        yre = y_tile
        ynf = bsb("ynf", [128, 8 * 1024])
        vaug = bsb("vaug", [128, 2048])
        prodb = [bsb("prod0", [128, 2048])]
        o1 = bsb("o1", [128, SL * D])
        sq_b = o1
        qkv_ts = bsb("qkv_ts", [128, SL * 1536])
        qf_tmp = bsb("qf_tmp", [128, 1024])
        wbig = bsb("wbig", [128, 4096])
        scob = bsb("scob", [128, 512])
        qe_b = bsb("qe_b", [128, 512])
        mx_b = bsb("mx_b", [128, 64])
        dx_b = bsb("dx_b", [128, 64])
        attb = bsb("attb", [128, 512])
        res_b = bsb("res_b", [128, 512])

        vde_rep = bsb("vde_rep", [128, SL * SL * 64])
        qk_w = int(qkv_ts.shape[1])
        slot_ev = {0: 0, 1: 0}
        for s in range(SL):
            for sp in range(SL):
                dd = min(abs(s - sp), 10)
                p.op("sync", lambda e, s=s, sp=sp, dd=dd: e.dma_start(
                    out=vde_rep[:, s * 512 + sp * 64: s * 512 + (sp + 1) * 64],
                    in_=vde_raw[dd:dd + 1, :].partition_broadcast(128)), inc=[(ld, 16)])

        for tt in range(2):
            tb0 = tt * 128
            for s in range(SL):
                p.op("sync", lambda e, s=s, tb0=tb0: e.dma_start(
                    out=y_tile[:], in_=ccout[s, tb0:tb0 + 128, :]),
                    wait=[(cc, None), (sc, None)], inc=[(ld, 16)])
                emit_ln(y_tile[:], qf_tmp[:], D, sq_b[:, 0:D])

                def tr(e):
                    last = None
                    for jj in range(8):
                        bank = BANKS[jj // 4]
                        last = e.transpose(
                            bank[:, (jj % 4) * 128:(jj % 4) * 128 + 128],
                            qf_tmp[:, jj * 128:(jj + 1) * 128], ident[:])
                    return last
                p.op("tensor", tr, wait=[(sc, None)], inc=[(mm, 1)])

                def trev(e, s=s):
                    last = None
                    for jj in range(8):
                        bank = BANKS[jj // 4]
                        last = e.activation(
                            ynf[:, jj * 1024 + s * 128: jj * 1024 + (s + 1) * 128],
                            bank[:, (jj % 4) * 128:(jj % 4) * 128 + 128], AF.Copy)
                    return last
                p.op("scalar", trev, wait=[(mm, None)], inc=[(sc, 1)])

            ynf_ready = sc.n
            attn_ve_done = ve.n
            av_done = {}
            qeadd_ve = ve.n
            evac_ev = {0: sc.n, 1: sc.n}

            for grp in range(2):
                qtrev_ev = sc.n
                for part in range(3):
                    for icl in range(4):
                        ic = part * 8 + grp * 4 + icl
                        wslot = ((part * 4 + icl) % 2) * 2048
                        bslot = wslot // 2048
                        for j in range(8):
                            p.op("sync", lambda e, ic=ic, j=j, wslot=wslot: e.dma_start(
                                out=wbig[:, wslot + j * 256: wslot + j * 256 + 128],
                                in_=winT[j * 128:(j + 1) * 128, ic * 128:(ic + 1) * 128]),
                                wait=[(mm, slot_ev[wslot // 2048]),
                                      (ve, attn_ve_done)], inc=[(ld, 16)])
                        p.op("sync", lambda e, ic=ic, bslot=bslot: e.dma_start(
                            out=b_in_sb[bslot * 32:bslot * 32 + 1, :],
                            in_=b_in[0:1, ic * 128:(ic + 1) * 128]), inc=[(ld, 16)])
                        dma_ev = ld.n
                        for rh in range(2):
                            bank = BANKS[rh]

                            def qkvmm(e, ic=ic, rh=rh, wslot=wslot, bank=bank):
                                for j in range(8):
                                    e.matmul(
                                        bank[:],
                                        wbig[:, wslot + j * 256: wslot + j * 256 + 128],
                                        ynf[:, j * 1024 + rh * 512: j * 1024 + (rh + 1) * 512],
                                        start=(j == 0), stop=False)
                                return e.matmul(
                                    bank[:],
                                    b_in_sb[(wslot // 2048) * 32:(wslot // 2048) * 32 + 1, :],
                                    ones_row[(wslot // 2048) * 32:(wslot // 2048) * 32 + 1, 0:512],
                                    start=False, stop=True)
                            p.op("tensor", qkvmm,
                                 wait=[(ld, dma_ev), (sc, max(ynf_ready, evac_ev[rh]))],
                                 inc=[(mm, 1)])
                            slot_ev[wslot // 2048] = mm.n
                            p.op("scalar", lambda e, rh=rh, bank=bank, part=part:
                                 e.activation(qf_tmp[:, rh * 512:(rh + 1) * 512],
                                              bank[:], AF.Copy, bias=0.0,
                                              scale=(0.125 if part == 0 else 1.0)),
                                 wait=[(mm, None)], inc=[(sc, 1)])
                            evac_ev[rh] = sc.n

                        if part == 0:
                            def qemm(e):
                                last = None
                                for hl in range(2):
                                    for s in range(SL):
                                        last = e.matmul(
                                            pb2[:, (hl * 8 + s) * 8:(hl * 8 + s) * 8 + 8],
                                            qf_tmp[hl * 64:(hl + 1) * 64,
                                                   s * 128:(s + 1) * 128],
                                            kdeT[hl * 64:(hl + 1) * 64, :],
                                            start=True, stop=True)
                                return last
                            p.op("tensor", qemm, wait=[(sc, evac_ev[1])], inc=[(mm, 1)])
                            p.op("scalar", lambda e, icl=icl: e.activation(
                                qe_b[:, icl * 128:(icl + 1) * 128], pb2[:, 0:128],
                                AF.Copy), wait=[(mm, None), (ve, qeadd_ve)],
                                inc=[(sc, 1)])

                        def qtr(e):
                            last = None
                            for rb in range(8):
                                bank = BANKS[4 + rb // 2]
                                last = e.transpose(
                                    bank[:, (rb % 2) * 128:(rb % 2) * 128 + 128],
                                    qf_tmp[:, rb * 128:(rb + 1) * 128], ident[:])
                            return last
                        p.op("tensor", qtr,
                             wait=[(sc, evac_ev[1]), (ve, attn_ve_done)], inc=[(mm, 1)])

                        def qtrev(e, part=part, icl=icl):
                            last = None
                            for rb in range(8):
                                bank = BANKS[4 + rb // 2]
                                col = rb * 1536 + part * 512 + icl * 128
                                last = e.activation(
                                    qkv_ts[:, col:col + 128],
                                    bank[:, (rb % 2) * 128:(rb % 2) * 128 + 128],
                                    AF.Copy)
                            return last
                        p.op("scalar", qtrev, wait=[(mm, None)], inc=[(sc, 1)])
                        qtrev_ev = sc.n

                for si in range(16):
                    s, hf = si // 2, si % 2
                    prod = prodb[0]

                    def qk(e, s=s, hf=hf, prod=prod):
                        in0 = bass.AP(qkv_ts, s * 1536 + hf * 256,
                                      [[qk_w, 128], [64, 4], [0, 8], [1, 64]])
                        in1 = bass.AP(qkv_ts, 512 + hf * 256,
                                      [[qk_w, 128], [64, 4], [1536, 8], [1, 64]])
                        e.tensor_tensor(
                            bass.AP(prod, 0, [[2048, 128], [512, 4], [64, 8], [1, 64]]),
                            in0, in1, ALU.mult)
                        out = bass.AP(scob, s * 8 + hf * 256,
                                      [[512, 128], [64, 4], [1, 8]])
                        return e.tensor_reduce(
                            out, bass.AP(prod, 0, [[2048, 128], [512, 4], [64, 8], [1, 64]]),
                            AX.X, ALU.add)
                    p.op("vector", qk, wait=[(sc, qtrev_ev)], inc=[(ve, 1)])

                def qe_add(e):
                    for s in range(SL):
                        w = SL - s
                        i0 = bass.AP(scob, s * 8 + s, [[512, 128], [64, 8], [1, w]])
                        i1 = bass.AP(qe_b, s * 8, [[512, 128], [64, 8], [1, w]])
                        e.tensor_tensor(i0, i0, i1, ALU.add)
                        e.drain()
                        if s > 0:
                            i0 = bass.AP(scob, s * 8, [[512, 128], [64, 8], [1, s]])
                            i1 = bass.AP(qe_b, s * 8 + s, [[512, 128], [64, 8], [-1, s]])
                            e.tensor_tensor(i0, i0, i1, ALU.add)
                            e.drain()
                    return e.tensor_reduce(
                        mx_b[:], scob[:].rearrange("p (a b) -> p a b", b=8),
                        AX.X, ALU.max)
                p.op("vector", qe_add, wait=[(sc, None)], inc=[(ve, 1)])
                qeadd_ve = ve.n

                def smx(e):
                    e.drain()
                    return e.tensor_tensor(
                        attb[:].rearrange("p (a b) -> p a b", b=8),
                        scob[:].rearrange("p (a b) -> p a b", b=8),
                        mx_b[:].unsqueeze(2).to_broadcast([128, 64, 8]),
                        ALU.subtract)
                p.op("vector", smx, inc=[(ve, 1)])
                p.op("scalar", lambda e: e.activation(attb[:], attb[:], AF.Exp),
                     wait=[(ve, None)], inc=[(sc, 1)])

                def nrm(e):
                    e.tensor_reduce(dx_b[:], attb[:].rearrange("p (a b) -> p a b", b=8),
                                    AX.X, ALU.add)
                    e.drain()
                    e.reciprocal(dx_b[:], dx_b[:])
                    e.drain()
                    return e.tensor_tensor(
                        attb[:].rearrange("p (a b) -> p a b", b=8),
                        attb[:].rearrange("p (a b) -> p a b", b=8),
                        dx_b[:].unsqueeze(2).to_broadcast([128, 64, 8]),
                        ALU.mult)
                p.op("vector", nrm, wait=[(sc, None)], inc=[(ve, 1)])

                for si in range(16):
                    s, hf = si // 2, si % 2
                    prod = prodb[0]

                    def av(e, s=s, grp=grp, hf=hf, prod=prod):
                        iv = bass.AP(qkv_ts, 1024 + hf * 256,
                                     [[qk_w, 128], [64, 4], [1536, 8], [1, 64]])
                        ivde = bass.AP(vde_rep, s * 512,
                                       [[int(vde_rep.shape[1]), 128], [0, 4], [64, 8], [1, 64]])
                        e.tensor_tensor(
                            bass.AP(vaug, 0, [[2048, 128], [512, 4], [64, 8], [1, 64]]),
                            iv, ivde, ALU.add)
                        ia = bass.AP(attb, s * 8 + hf * 256,
                                     [[512, 128], [64, 4], [0, 64], [1, 8]])
                        ivg = bass.AP(vaug, 0, [[2048, 128], [512, 4], [1, 64], [64, 8]])
                        e.tensor_tensor(
                            bass.AP(prod, 0, [[2048, 128], [512, 4], [8, 64], [1, 8]]),
                            ia, ivg, ALU.mult)
                        out = bass.AP(o1, s * D + grp * 512 + hf * 256,
                                      [[SL * D, 128], [64, 4], [1, 64]])
                        return e.tensor_reduce(
                            out, bass.AP(prod, 0, [[2048, 128], [512, 4], [8, 64], [1, 8]]),
                            AX.X, ALU.add)
                    p.op("vector", av, inc=[(ve, 1)])
                    if hf == 1:
                        av_done[(grp, s)] = ve.n
                attn_ve_done = ve.n

            o1f = qkv_ts
            qkv_mm_done = mm.n
            o1trev_ev = {}
            for s in range(SL):
                def o1tr(e, s=s):
                    last = None
                    for jj in range(8):
                        bank = BANKS[2 + jj // 4]
                        last = e.transpose(
                            bank[:, (jj % 4) * 128:(jj % 4) * 128 + 128],
                            o1[:, s * D + jj * 128: s * D + (jj + 1) * 128], ident[:])
                    return last
                p.op("tensor", o1tr,
                     wait=[(ve, max(av_done[(0, s)], av_done[(1, s)])), (sc, None)],
                     inc=[(mm, 1)])

                def o1trev(e, s=s):
                    last = None
                    for jj in range(8):
                        bank = BANKS[2 + jj // 4]
                        last = e.activation(
                            o1f[:, jj * 1024 + s * 128: jj * 1024 + (s + 1) * 128],
                            bank[:, (jj % 4) * 128:(jj % 4) * 128 + 128], AF.Copy)
                    return last
                p.op("scalar", o1trev, wait=[(mm, None), (ve, attn_ve_done)],
                     inc=[(sc, 1)])
                o1trev_ev[s] = sc.n

            s1 = o1
            for ih in range(2):
                for j in range(8):
                    p.op("sync", lambda e, j=j, ih=ih: e.dma_start(
                        out=wbig[:, j * 512:(j + 1) * 512],
                        in_=woutT[j * 128:(j + 1) * 128, ih * 512:ih * 512 + 512]),
                        wait=[(mm, qkv_mm_done if ih == 0 else mm.n)], inc=[(ld, 16)])
                wo_ld = ld.n
                for rc in range(SL):
                    bank = BANKS[rc % 2]

                    def s1mm(e, rc=rc, ih=ih, bank=bank):
                        for j in range(8):
                            e.matmul(bank[:],
                                     o1f[:, j * 1024 + rc * 128: j * 1024 + (rc + 1) * 128],
                                     wbig[:, j * 512:(j + 1) * 512],
                                     start=(j == 0), stop=False)
                        return e.matmul(bank[:], ones_row[0:1, 0:128],
                                        b_out_sb[0:1, ih * 512:(ih + 1) * 512],
                                        start=False, stop=True)
                    p.op("tensor", s1mm,
                         wait=[(ld, wo_ld), (sc, o1trev_ev[rc] if ih == 0 else sc.n)],
                         inc=[(mm, 1)])
                    p.op("scalar", lambda e, rc=rc, ih=ih, bank=bank: e.activation(
                        s1[:, rc * D + ih * 512: rc * D + (ih + 1) * 512],
                        bank[:], AF.Copy),
                        wait=[(mm, None)], inc=[(sc, 1)])

            sq2_ev = {0: sc.n, 1: sc.n}
            add_ev = {0: ve.n, 1: ve.n}
            gsil_ev = {0: sc.n, 1: sc.n}
            fin_ev = {0: ve.n, 1: ve.n}
            odma_ev = {0: g.n, 1: g.n}
            yld_ev = {0: ld.n, 1: ld.n}
            for oh in range(2):
                for j in range(8):
                    p.op("sync", lambda e, j=j, oh=oh: e.dma_start(
                        out=wbig[:, j * 512:(j + 1) * 512],
                        in_=cwscr[j * 128:(j + 1) * 128, oh * 512:(oh + 1) * 512]),
                        wait=[(mm, None)], inc=[(ld, 16)])
                    p.op("sync", lambda e, j=j, oh=oh: e.dma_start(
                        out=qkv_ts[:, 8192 + j * 512: 8192 + (j + 1) * 512],
                        in_=swscr[j * 128:(j + 1) * 128, oh * 512:(oh + 1) * 512]),
                        wait=[(ve, attn_ve_done)], inc=[(ld, 16)])
                w_ld = ld.n
                for rc in range(SL):
                    par = rc % 2
                    b0, b1 = BANKS[par * 2], BANKS[par * 2 + 1]
                    fr = vaug[:, par * 1024: par * 1024 + 512]
                    fi = vaug[:, par * 1024 + 512: par * 1024 + 1024]
                    yslot = y_tile[:, par * 512:(par + 1) * 512]

                    def ffnmm(e, rc=rc, b0=b0, b1=b1):
                        for j in range(8):
                            e.matmul(b0[:],
                                     ynf[:, j * 1024 + rc * 128: j * 1024 + (rc + 1) * 128],
                                     wbig[:, j * 512:(j + 1) * 512],
                                     start=(j == 0), stop=(j == 7))
                        last = None
                        for j in range(8):
                            last = e.matmul(
                                b1[:],
                                ynf[:, j * 1024 + rc * 128: j * 1024 + (rc + 1) * 128],
                                qkv_ts[:, 8192 + j * 512: 8192 + (j + 1) * 512],
                                start=(j == 0), stop=(j == 7))
                        return last
                    p.op("tensor", ffnmm, wait=[(ld, w_ld), (sc, sq2_ev[par])],
                         inc=[(mm, 1)])
                    mm_ev = mm.n

                    def sq2(e, fr=fr, fi=fi, b0=b0, b1=b1):
                        e.activation(fr, b0[:], AF.Square)
                        return e.activation(fi, b1[:], AF.Square)
                    p.op("scalar", sq2, wait=[(mm, mm_ev), (ve, add_ev[par]),
                                               (g, odma_ev[par])],
                         inc=[(sc, 1)])
                    sq2_ev[par] = sc.n
                    p.op("vector", lambda e, fr=fr, fi=fi: e.tensor_add(fr, fr, fi),
                         wait=[(sc, sq2_ev[par]), (ve, fin_ev[par])], inc=[(ve, 1)])
                    add_ev[par] = ve.n

                    def gsil(e, fr=fr):
                        e.activation(fr, fr, AF.Sqrt)
                        return e.activation(fr, fr, AF.Silu)
                    p.op("scalar", gsil, wait=[(ve, add_ev[par])], inc=[(sc, 1)])
                    gsil_ev[par] = sc.n
                    p.op("sync", lambda e, rc=rc, oh=oh, tb0=tb0, yslot=yslot: e.dma_start(
                        out=yslot, in_=ccout[rc, tb0:tb0 + 128, oh * 512:(oh + 1) * 512]),
                        wait=[(ve, fin_ev[par])], inc=[(ld, 16)])
                    yld_ev[par] = ld.n

                    def fin(e, rc=rc, oh=oh, fr=fr, yslot=yslot):
                        e.tensor_mul(fr, fr,
                                     s1[:, rc * D + oh * 512: rc * D + (oh + 1) * 512])
                        return e.tensor_add(fr, fr, yslot)
                    p.op("vector", fin, wait=[(sc, gsil_ev[par]), (g, odma_ev[par]),
                                              (ld, yld_ev[par])], inc=[(ve, 1)])
                    fin_ev[par] = ve.n
                    p.op("gpsimd", lambda e, rc=rc, oh=oh, tb0=tb0, fr=fr: e.dma_start(
                        out=out_ext[rc, tb0:tb0 + 128, oh * 512:(oh + 1) * 512],
                        in_=fr),
                        wait=[(ve, fin_ev[par])], inc=[(g, 16)])
                    odma_ev[par] = g.n
            p.fence(CHANS)
        br.close()

    a_pass1()
    ofs = ExitStack()
    o_full = ofs.enter_context(nc.sbuf_tensor("o_full", [128, TC * D], F32))
    a_attn(o_full)
    a_pass3(o_full)
    ofs.close()
    a2a_and_trig()
    phase_b()
    p.fence(CHANS)
    p.emit()
    es.close()
    return nc


_NC_CACHE = None


def _get_nc():
    global _NC_CACHE
    if _NC_CACHE is None:
        _NC_CACHE = build_program()
    return _NC_CACHE


def kernel(L0_attn_x, L0_attn_Wq, L0_attn_Wk, L0_attn_Wv, L0_attn_norm_w,
           L0_attn_Wo, L0_attn_bo, L0_attn_diff_weight, L0_ffn_w_route,
           L0_ffn_compress, L0_ffn_routed, L1_attn_W_in, L1_attn_b_in,
           L1_attn_k_dist_emb, L1_attn_v_dist_emb, L1_attn_W_out, L1_attn_b_out,
           L1_ffn_theta, L1_ffn_phi, L1_ffn_theta_noise, L1_ffn_phi_noise):
    f = np.float32
    c = np.ascontiguousarray
    x = c(np.asarray(L0_attn_x, f))
    shared = {
        "wqT": c(np.asarray(L0_attn_Wq, f).T),
        "wkT": c(np.asarray(L0_attn_Wk, f).T),
        "wvT": c(np.asarray(L0_attn_Wv, f).T),
        "woT": c(np.asarray(L0_attn_Wo, f).T),
        "nw_col": c(np.asarray(L0_attn_norm_w, f).reshape(8, 128).T),
        "bo": c(np.asarray(L0_attn_bo, f).reshape(1, D)),
        "dw_in": c(np.asarray(L0_attn_diff_weight, f).reshape(1, 1)),
        "wg": c(np.concatenate([np.asarray(L0_ffn_w_route, f).T,
                                np.asarray(L0_ffn_compress, f).T], axis=1)),
        "r2": c(np.asarray(L0_ffn_routed, f).transpose(0, 2, 1).reshape(128, D)),
        "winT": c(np.asarray(L1_attn_W_in, f).T),
        "b_in": c(np.asarray(L1_attn_b_in, f).reshape(1, 3 * D)),
        "kde_raw": c(np.asarray(L1_attn_k_dist_emb, f)),
        "vde_raw": c(np.asarray(L1_attn_v_dist_emb, f)),
        "woutT": c(np.asarray(L1_attn_W_out, f).T),
        "b_out": c(np.asarray(L1_attn_b_out, f).reshape(1, D)),
        "theta": c(np.asarray(L1_ffn_theta, f)),
        "thn": c(np.asarray(L1_ffn_theta_noise, f)),
        "phi": c(np.asarray(L1_ffn_phi, f)),
        "phn": c(np.asarray(L1_ffn_phi_noise, f)),
        "ident": np.eye(128, dtype=f),
    }
    in_maps = [dict(shared, xb=c(x[i])) for i in range(NCORE)]
    nc = _get_nc()
    res = run_bass_kernel_spmd(nc, in_maps, core_ids=list(range(NCORE)))
    return np.concatenate([res.results[i]["out"] for i in range(NCORE)], axis=1)


# version 25
# speedup vs baseline: 1.0142x; 1.0142x over previous
import numpy as np
from contextlib import ExitStack

import concourse.bass as bass
import concourse.mybir as mybir
from concourse.bass_utils import run_bass_kernel_spmd
from concourse.aot_env import allow_ici_all_to_all

F32 = mybir.dt.float32
AF = mybir.ActivationFunctionType
ALU = mybir.AluOpType
AX = mybir.AxisListType

NCORE = 8
S = 2048
D = 1024
H = 8
TC = 16
TLOC = 256
SL = 8
RMS_EPS = float(np.float32(1.1920929e-07))
LN_EPS = 1e-5
PI_HALF = float(np.pi / 2)


class Ch:
    def __init__(self, nc, es, name):
        self.sem = es.enter_context(nc.semaphore(name))
        self.n = 0


class Prog:
    ENGS = ("sync", "tensor", "vector", "scalar", "gpsimd")

    def __init__(self, nc):
        self.nc = nc
        self.ops = {e: [] for e in self.ENGS}

    def op(self, eng, fn, wait=(), inc=()):
        rw = [(ch, ch.n if v is None else v) for ch, v in wait]
        ri = []
        for ch, n in inc:
            ch.n += n
            ri.append((ch, n))
        self.ops[eng].append((fn, rw, ri))

    def fence(self, chans):
        vals = [(ch, ch.n) for ch in chans]
        for e in self.ENGS:
            self.ops[e].append((None, list(vals), []))

    def emit(self):
        with self.nc.Block() as block:
            for eng in self.ENGS:
                ops = self.ops[eng]
                if not ops:
                    continue

                def body(e, ops=ops):
                    lastw = {}
                    for fn, rw, ri in ops:
                        for ch, v in rw:
                            if v > 0 and lastw.get(id(ch), -1) < v:
                                e.wait_ge(ch.sem, v)
                                lastw[id(ch)] = v
                        if fn is None:
                            continue
                        instr = fn(e)
                        for ch, n in ri:
                            instr.then_inc(ch.sem, n)

                getattr(block, eng)(body)


def apx(handle, off, dims):
    w = int(handle.shape[1])
    return bass.AP(handle, off, [[w, 128]] + [list(d) for d in dims])


def build_program():
    nc = bass.Bass("TRN2", target_bir_lowering=False, debug=False,
                   detect_race_conditions=False)

    def din(name, shape):
        return nc.dram_tensor(name, shape, F32, kind="ExternalInput").ap()

    def dscr(name, shape):
        return nc.dram_tensor(name, shape, F32).ap()

    xb = din("xb", [S, D])
    wqT = din("wqT", [D, D])
    wkT = din("wkT", [D, D])
    wvT = din("wvT", [D, D])
    woT = din("woT", [D, D])
    nw_col = din("nw_col", [128, 8])
    bo = din("bo", [1, D])
    dw_in = din("dw_in", [1, 1])
    wg = din("wg", [D, 24])
    r2 = din("r2", [128, D])
    winT = din("winT", [D, 3 * D])
    b_in = din("b_in", [1, 3 * D])
    kde_raw = din("kde_raw", [11, 64])
    vde_raw = din("vde_raw", [11, 64])
    woutT = din("woutT", [D, D])
    b_out = din("b_out", [1, D])
    theta = din("theta", [D, D])
    thn = din("thn", [D, D])
    phi = din("phi", [D, D])
    phn = din("phn", [D, D])
    ident_in = din("ident", [128, 128])
    out_ext = nc.dram_tensor("out", [SL, TLOC, D], F32, kind="ExternalOutput").ap()

    qscr = dscr("qscr", [D, S])
    kscr = dscr("kscr", [D, S])
    vscr = dscr("vscr", [H, S, 128])
    ccin = dscr("ccin", [S, D])
    ccout = dscr("ccout", [SL, TLOC, D])
    cwscr = dscr("cwscr", [D, D])
    swscr = dscr("swscr", [D, D])

    es = ExitStack()
    sb = lambda name, shape: es.enter_context(nc.sbuf_tensor(name, shape, F32))

    ld = Ch(nc, es, "ld")
    g = Ch(nc, es, "g")
    gc = Ch(nc, es, "gc")
    cc = Ch(nc, es, "cc")
    mm = Ch(nc, es, "mm")
    ve = Ch(nc, es, "ve")
    sc = Ch(nc, es, "sc")
    CHANS = [ld, g, gc, cc, mm, ve, sc]

    p = Prog(nc)

    pS = es.enter_context(nc.psum_tensor("pS", [128, 2048], F32))
    pT = es.enter_context(nc.psum_tensor("pT", [128, 2048], F32))

    class BankView:
        def __init__(self, parent, off):
            self.parent, self.off = parent, off

        def __getitem__(self, idx):
            if idx == slice(None):
                return self.parent[:, self.off:self.off + 512]
            ps, fs = idx
            a = fs.start or 0
            b = fs.stop if fs.stop is not None else 512
            return self.parent[ps, self.off + a:self.off + b]

    BANKS = [BankView(pS, i * 512) for i in range(4)] +             [BankView(pT, i * 512) for i in range(4)]
    pb0, pb1, pb2, pb3, pb4, pb5, pb6, pb7 = BANKS

    ident = sb("ident_sb", [128, 128])
    ones_row = sb("ones_row", [33, 512])
    bo_sb = sb("bo_sb", [1, D])
    b_in_sb = sb("b_in_sb", [33, 128])
    b_out_sb = sb("b_out_sb", [1, D])
    nw_sb = sb("nw_sb", [128, 8])
    dw_sb = sb("dw_sb", [1, 1])
    dwb = sb("dwb", [128, 1])
    ndw = sb("ndw", [128, 1])
    omdw = sb("omdw", [128, 1])
    kdeT = sb("kdeT", [128, 8])
    gates_buf = sb("gates_buf", [128, TC * 24])
    ln_t = [sb(f"ln{i}", [128, 1]) for i in range(10)]
    eps_ln = sb("eps_ln", [128, 1])
    eps_rms = sb("eps_rms", [128, 1])
    pih_sb = sb("pih_sb", [128, 1])

    p.op("sync", lambda e: e.dma_start(out=ident[:], in_=ident_in), inc=[(ld, 16)])
    p.op("sync", lambda e: e.dma_start(out=bo_sb[:], in_=bo), inc=[(ld, 16)])
    p.op("sync", lambda e: e.dma_start(out=b_out_sb[:], in_=b_out), inc=[(ld, 16)])
    p.op("sync", lambda e: e.dma_start(out=nw_sb[:], in_=nw_col), inc=[(ld, 16)])
    p.op("sync", lambda e: e.dma_start(out=dw_sb[:], in_=dw_in), inc=[(ld, 16)])
    kde8 = sb("kde8", [8, 64])
    p.op("sync", lambda e: e.dma_start(out=kde8[:], in_=kde_raw[0:8, :]), inc=[(ld, 16)])
    def _consts(e):
        e.memset(ones_row[:], 1.0)
        e.memset(eps_ln[:], LN_EPS)
        e.memset(eps_rms[:], RMS_EPS)
        return e.memset(pih_sb[:], PI_HALF)
    p.op("vector", _consts, inc=[(ve, 1)])
    p.op("tensor", lambda e: e.matmul(pb0[:, 0:1], ones_row[0:1, 0:128], dw_sb[:],
                                      start=True, stop=True),
         wait=[(ld, None), (ve, None)], inc=[(mm, 1)])

    def _dw(e):
        e.tensor_copy(dwb[:], pb0[:, 0:1])
        e.drain()
        e.tensor_scalar(ndw[:], dwb[:], -1.0, None, ALU.mult)
        e.drain()
        return e.tensor_scalar(omdw[:], dwb[:], -1.0, 1.0, ALU.mult, ALU.add)
    p.op("vector", _dw, wait=[(mm, None)], inc=[(ve, 1)])
    p.op("tensor", lambda e: e.transpose(pb1[0:64, 0:8], kde8[:], ident[0:8, 0:8]),
         wait=[(ld, None)], inc=[(mm, 1)])
    p.op("scalar", lambda e: e.activation(kdeT[0:64, :], pb1[0:64, 0:8], AF.Copy),
         wait=[(mm, None)], inc=[(sc, 1)])
    p.op("gpsimd", lambda e: e.dma_start(out=kdeT[64:128, :], in_=kdeT[0:64, :]),
         wait=[(sc, None)], inc=[(g, 16)])

    def emit_ln(x_ap, xn_ap, width, sqbuf):
        inv = 1.0 / width

        def v1(e):
            e.reduce_sum(ln_t[0][:], x_ap, axis=AX.X)
            e.drain()
            return e.tensor_scalar(ln_t[1][:], ln_t[0][:], inv, None, ALU.mult)
        p.op("vector", v1, wait=[(ld, None), (cc, None)], inc=[(ve, 1)])
        p.op("scalar", lambda e: e.activation(sqbuf, x_ap, AF.Square),
             wait=[(ld, None), (ve, None), (cc, None)], inc=[(sc, 1)])

        def v2(e):
            e.reduce_sum(ln_t[2][:], sqbuf, axis=AX.X)
            e.drain()
            e.tensor_scalar(ln_t[3][:], ln_t[2][:], inv, None, ALU.mult)
            e.tensor_mul(ln_t[4][:], ln_t[1][:], ln_t[1][:])
            e.drain()
            return e.tensor_sub(ln_t[5][:], ln_t[3][:], ln_t[4][:])
        p.op("vector", v2, wait=[(sc, None)], inc=[(ve, 1)])
        p.op("scalar", lambda e: e.activation(ln_t[6][:], ln_t[5][:], AF.Sqrt,
                                              bias=eps_ln[:], scale=1.0),
             wait=[(ve, None)], inc=[(sc, 1)])

        def v3(e):
            e.reciprocal(ln_t[7][:], ln_t[6][:])
            e.drain()
            e.tensor_scalar(ln_t[8][:], ln_t[1][:], -1.0, None, ALU.mult)
            e.drain()
            return e.tensor_scalar(xn_ap, x_ap, ln_t[8][:], ln_t[7][:],
                                   ALU.add, ALU.mult)
        p.op("vector", v3, wait=[(sc, None), (mm, None)], inc=[(ve, 1)])
        p.op("scalar", lambda e: e.activation(ln_t[0][:], ln_t[7][:], AF.Copy),
             wait=[(ve, None)], inc=[(sc, 1)])

    def a_pass1():
        ar = ExitStack()
        asb = lambda n, s: ar.enter_context(nc.sbuf_tensor(n, s, F32))
        wq_sb = asb("wq_sb", [128, 8 * D])
        wk_sb = asb("wk_sb", [128, 8 * D])
        wv_sb = asb("wv_sb", [128, 8 * D])
        wg_sb = asb("wg_sb", [128, 8 * 24])
        x_tc = asb("x_tc", [128, D])
        sq_tc = asb("sq_tc", [128, D])
        xn_tc = asb("xn_tc", [128, D])
        xnf = asb("xnf", [128, 8 * 512])
        qkbuf = asb("qkbuf", [128, 1024])
        vbuf = asb("vbuf", [128, D])

        for j in range(8):
            for wsb, src_ in ((wq_sb, wqT), (wk_sb, wkT), (wv_sb, wvT)):
                p.op("sync", lambda e, j=j, wsb=wsb, src_=src_: e.dma_start(
                    out=wsb[:, j * D:(j + 1) * D],
                    in_=src_[j * 128:(j + 1) * 128, :]), inc=[(ld, 16)])
            p.op("sync", lambda e, j=j: e.dma_start(
                out=wg_sb[:, j * 24:(j + 1) * 24],
                in_=wg[j * 128:(j + 1) * 128, :]), inc=[(ld, 16)])

        for sci in range(4):
            t0s = sci * 512
            for sub in range(4):
                t0 = t0s + sub * 128
                p.op("sync", lambda e, t0=t0: e.dma_start(out=x_tc[:], in_=xb[t0:t0 + 128, :]),
                     wait=[(sc, None)], inc=[(ld, 16)])
                emit_ln(x_tc[:], xn_tc[:], D, sq_tc[:])
                bank0 = BANKS[4 + 2 * (sub % 2)]
                bank1 = BANKS[5 + 2 * (sub % 2)]

                def tr(e, bank0=bank0, bank1=bank1):
                    last = None
                    for jj in range(8):
                        bank = bank0 if jj < 4 else bank1
                        last = e.transpose(bank[:, (jj % 4) * 128:(jj % 4) * 128 + 128],
                                           xn_tc[:, jj * 128:(jj + 1) * 128], ident[:])
                    return last
                p.op("tensor", tr, wait=[(sc, None)], inc=[(mm, 1)])

                def trev(e, sub=sub, bank0=bank0, bank1=bank1):
                    o0 = bass.AP(xnf, sub * 128, [[4096, 128], [512, 4], [1, 128]])
                    e.activation(o0, bank0[:], AF.Copy)
                    o1_ = bass.AP(xnf, 4 * 512 + sub * 128, [[4096, 128], [512, 4], [1, 128]])
                    return e.activation(o1_, bank1[:], AF.Copy)
                p.op("scalar", trev, wait=[(mm, None)], inc=[(sc, 1)])

            qk_slot_ev = getattr(a_pass1, "_slot_ev", {0: 0, 512: 0})
            for wsb, scr, qsc in ((wq_sb, qscr, 0.125), (wk_sb, kscr, 1.0)):
                for ic in range(8):
                    bank = BANKS[ic % 4]
                    slot = (ic % 2) * 512

                    def mm1(e, wsb=wsb, ic=ic, bank=bank):
                        last = None
                        for j in range(8):
                            last = e.matmul(
                                bank[:],
                                wsb[:, j * D + ic * 128: j * D + (ic + 1) * 128],
                                xnf[:, j * 512:(j + 1) * 512],
                                start=(j == 0), stop=(j == 7))
                        return last
                    p.op("tensor", mm1, wait=[(sc, None)], inc=[(mm, 1)])
                    p.op("scalar", lambda e, bank=bank, slot=slot, qsc=qsc: e.activation(
                        qkbuf[:, slot:slot + 512], bank[:], AF.Copy,
                        bias=0.0, scale=qsc),
                        wait=[(mm, None), (g, qk_slot_ev[slot])], inc=[(sc, 1)])
                    p.op("gpsimd", lambda e, ic=ic, scr=scr, t0s=t0s, slot=slot: e.dma_start(
                        out=scr[ic * 128:(ic + 1) * 128, t0s:t0s + 512],
                        in_=qkbuf[:, slot:slot + 512]),
                        wait=[(sc, None)], inc=[(g, 16)])
                    qk_slot_ev[slot] = g.n
            a_pass1._slot_ev = qk_slot_ev

            for sub in range(4):
                for vh in range(2):
                    bank = BANKS[4 + vh]

                    def mmv(e, sub=sub, vh=vh, bank=bank):
                        last = None
                        for j in range(8):
                            last = e.matmul(
                                bank[:],
                                xnf[:, j * 512 + sub * 128: j * 512 + sub * 128 + 128],
                                wv_sb[:, j * D + vh * 512: j * D + (vh + 1) * 512],
                                start=(j == 0), stop=(j == 7))
                        return last
                    p.op("tensor", mmv, wait=[(sc, None)], inc=[(mm, 1)])
                    p.op("scalar", lambda e, vh=vh, bank=bank: e.activation(
                        vbuf[:, vh * 512:(vh + 1) * 512], bank[:], AF.Copy),
                        wait=[(mm, None), (g, None)], inc=[(sc, 1)])
                for h in range(H):
                    p.op("gpsimd", lambda e, h=h, t0s=t0s, sub=sub: e.dma_start(
                        out=vscr[h, t0s + sub * 128:t0s + (sub + 1) * 128, :],
                        in_=vbuf[:, h * 128:(h + 1) * 128]),
                        wait=[(sc, None)], inc=[(g, 16)])

                def mmg(e, sub=sub):
                    last = None
                    for j in range(8):
                        last = e.matmul(pb6[:, 448:472],
                                        xnf[:, j * 512 + sub * 128: j * 512 + sub * 128 + 128],
                                        wg_sb[:, j * 24:(j + 1) * 24],
                                        start=(j == 0), stop=(j == 7))
                    return last
                p.op("tensor", mmg, wait=[(sc, None)], inc=[(mm, 1)])
                p.op("scalar", lambda e, sub=sub, t0s=t0s: e.activation(
                    gates_buf[:, (t0s // 128 + sub) * 24:(t0s // 128 + sub + 1) * 24],
                    pb6[:, 448:472], AF.Copy),
                    wait=[(mm, None)], inc=[(sc, 1)])

        p.fence(CHANS)
        ar.close()

    def a_attn(o_full):
        ar = ExitStack()
        asb = lambda n, s: ar.enter_context(nc.sbuf_tensor(n, s, F32))
        qp0b = [asb(f"qp0{i}", [128, S]) for i in range(2)]
        qp1b = [asb(f"qp1{i}", [128, S]) for i in range(2)]
        khb = [asb(f"kh{i}", [128, S]) for i in range(2)]
        def _zpad(e):
            e.memset(qp0b[0][64:128, :], 0.0)
            e.memset(qp0b[1][64:128, :], 0.0)
            e.memset(qp1b[0][0:64, :], 0.0)
            return e.memset(qp1b[1][0:64, :], 0.0)
        p.op("vector", _zpad, inc=[(ve, 1)])
        vhb = [asb(f"vh{i}", [128, S]) for i in range(2)]
        e0 = asb("e0", [128, S])
        e1 = asb("e1", [128, S])
        att = asb("att", [128, S])
        attT = asb("attT", [128, S])
        d0 = asb("d0", [128, 1])
        d1 = asb("d1", [128, 1])
        d0a = asb("d0a", [128, 1])
        d0b = asb("d0b", [128, 1])
        d1a = asb("d1a", [128, 1])
        d1b = asb("d1b", [128, 1])
        a0s = [asb(f"a0v{i}", [128, 1]) for i in range(2)]
        c1 = asb("c1", [128, 1])
        m0 = asb("m0", [128, 1])
        m1 = asb("m1", [128, 1])
        t6 = asb("t6", [128, 1])
        t7 = asb("t7", [128, 1])

        def rec_loads(h, wait_mm):
            par = h % 2
            p.op("sync", lambda e, h=h, par=par: e.dma_start(
                out=qp0b[par][0:64, :], in_=qscr[h * 128:h * 128 + 64, :]),
                wait=[(mm, wait_mm)], inc=[(ld, 16)])
            p.op("sync", lambda e, h=h, par=par: e.dma_start(
                out=qp1b[par][64:128, :], in_=qscr[h * 128 + 64:(h + 1) * 128, :]),
                inc=[(ld, 16)])
            p.op("sync", lambda e, h=h, par=par: e.dma_start(
                out=khb[par][:], in_=kscr[h * 128:(h + 1) * 128, :]), inc=[(ld, 16)])
            for kc in range(16):
                p.op("sync", lambda e, h=h, kc=kc, par=par: e.dma_start(
                    out=vhb[par][:, kc * 128:(kc + 1) * 128],
                    in_=vscr[h, kc * 128:(kc + 1) * 128, :]), inc=[(ld, 16)])
            return ld.n

        head_done_mm = {-1: mm.n}
        ld_head = {0: rec_loads(0, mm.n)}

        for h in range(H):
            par = h % 2
            qp0, qp1, kh, vh = qp0b[par], qp1b[par], khb[par], vhb[par]
            base_sc, base_ve, base_mm = sc.n, ve.n, mm.n
            ev = {}
            pv = {}

            def emit_tail1(pq, pv):
                def tr(e):
                    last = None
                    for kc in range(16):
                        last = e.transpose(pT[:, kc * 128:(kc + 1) * 128],
                                           att[:, kc * 128:(kc + 1) * 128], ident[:])
                    return last
                p.op("tensor", tr, wait=[(ve, pv["combine"]),
                                         (sc, pv.get("attev_m2", base_sc)),
                                         (ve, pv.get("oevac_m2", base_ve))], inc=[(mm, 1)])
                pv["tr"] = mm.n
                p.op("scalar", lambda e: e.activation(attT[:], pT[:], AF.Copy),
                     wait=[(mm, max(pv["tr"], pv.get("av_m1", base_mm)))], inc=[(sc, 1)])
                pv["attev"] = sc.n

            def emit_tail2(pq, pv):
                def av(e, vh=vh):
                    last = None
                    for kc in range(16):
                        last = e.matmul(pT[:, 0:128],
                                        attT[:, kc * 128:(kc + 1) * 128],
                                        vh[:, kc * 128:(kc + 1) * 128],
                                        start=(kc == 0), stop=(kc == 15))
                    return last
                p.op("tensor", av, wait=[(sc, pv["attev"])], inc=[(mm, 1)])
                pv["av"] = mm.n
                p.op("vector", lambda e, h=h, pq=pq: e.tensor_scalar(
                    o_full[:, pq * D + h * 128: pq * D + (h + 1) * 128],
                    pT[:, 0:128], a0s[pq % 2][:], None, ALU.mult),
                    wait=[(mm, pv["av"])], inc=[(ve, 1)])
                pv["oevac"] = ve.n

            for qc in range(16):
                q0 = qc * 128
                def s0a(e, q0=q0, qp0=qp0, kh=kh):
                    last = None
                    for kb in range(2):
                        last = e.matmul(pS[:, kb * 512:(kb + 1) * 512],
                                        qp0[:, q0:q0 + 128],
                                        kh[:, kb * 512:(kb + 1) * 512],
                                        start=True, stop=True)
                    return last
                p.op("tensor", s0a,
                     wait=[(ld, ld_head[h]), (sc, pv.get("exp1a", base_sc))], inc=[(mm, 1)])

                def s0b(e, q0=q0, qp0=qp0, kh=kh):
                    last = None
                    for kb in range(2, 4):
                        last = e.matmul(pS[:, kb * 512:(kb + 1) * 512],
                                        qp0[:, q0:q0 + 128],
                                        kh[:, kb * 512:(kb + 1) * 512],
                                        start=True, stop=True)
                    return last
                p.op("tensor", s0b,
                     wait=[(sc, pv.get("exp1b", base_sc))], inc=[(mm, 1)])
                ev["s0"] = mm.n

                def max0(e):
                    return e.reduce_max(m0[:], pS[:], axis=AX.X, negate=True)
                p.op("vector", max0, wait=[(mm, ev["s0"])], inc=[(ve, 1)])
                ev["max0"] = ve.n

                p.op("scalar", lambda e: e.activation(e0[:, 0:1024], pS[:, 0:1024],
                                                      AF.Exp, bias=m0[:], scale=1.0,
                                                      accum_out=d0a[:]),
                     wait=[(ve, ev["max0"])], inc=[(sc, 1)])
                ev["exp0a"] = sc.n
                p.op("scalar", lambda e: e.activation(e0[:, 1024:2048], pS[:, 1024:2048],
                                                      AF.Exp, bias=m0[:], scale=1.0,
                                                      accum_out=d0b[:]),
                     inc=[(sc, 1)])
                ev["exp0"] = sc.n

                if pv:
                    emit_tail1(qc - 1, pv)

                def s1a(e, q0=q0, qp1=qp1, kh=kh):
                    last = None
                    for kb in range(2):
                        last = e.matmul(pS[:, kb * 512:(kb + 1) * 512],
                                        qp1[:, q0:q0 + 128],
                                        kh[:, kb * 512:(kb + 1) * 512],
                                        start=True, stop=True)
                    return last
                p.op("tensor", s1a, wait=[(sc, ev["exp0a"])], inc=[(mm, 1)])

                def s1b(e, q0=q0, qp1=qp1, kh=kh):
                    last = None
                    for kb in range(2, 4):
                        last = e.matmul(pS[:, kb * 512:(kb + 1) * 512],
                                        qp1[:, q0:q0 + 128],
                                        kh[:, kb * 512:(kb + 1) * 512],
                                        start=True, stop=True)
                    return last
                p.op("tensor", s1b, wait=[(sc, ev["exp0"])], inc=[(mm, 1)])
                ev["s1"] = mm.n

                def max1(e):
                    return e.reduce_max(m1[:], pS[:], axis=AX.X, negate=True)
                p.op("vector", max1, wait=[(mm, ev["s1"])], inc=[(ve, 1)])
                ev["max1"] = ve.n

                p.op("scalar", lambda e: e.activation(e1[:, 0:1024], pS[:, 0:1024],
                                                      AF.Exp, bias=m1[:], scale=1.0,
                                                      accum_out=d1a[:]),
                     wait=[(ve, ev["max1"])], inc=[(sc, 1)])
                ev["exp1a"] = sc.n
                p.op("scalar", lambda e: e.activation(e1[:, 1024:2048], pS[:, 1024:2048],
                                                      AF.Exp, bias=m1[:], scale=1.0,
                                                      accum_out=d1b[:]),
                     inc=[(sc, 1)])
                ev["exp1b"] = sc.n
                ev["exp1"] = sc.n

                if pv:
                    emit_tail2(qc - 1, pv)

                def comb(e, qc=qc):
                    e.tensor_add(d0[:], d0a[:], d0b[:])
                    e.tensor_add(d1[:], d1a[:], d1b[:])
                    e.drain()
                    e.reciprocal(a0s[qc % 2][:], d0[:])
                    e.reciprocal(t6[:], d1[:])
                    e.drain()
                    e.tensor_mul(t7[:], d0[:], t6[:])
                    e.drain()
                    e.tensor_mul(c1[:], t7[:], ndw[:])
                    e.drain()
                    return e.scalar_tensor_tensor(att[:], e1[:], c1[:], e0[:],
                                                  ALU.mult, ALU.add)
                p.op("vector", comb,
                     wait=[(sc, ev["exp1"])] + ([(mm, pv["tr"])] if pv else []),
                     inc=[(ve, 1)])
                ev["comb_g"] = gc.n
                ev["combine"] = ve.n

                ev["attev_m2"] = pv.get("attev", 0)
                ev["oevac_m2"] = pv.get("oevac", 0)
                ev["av_m1"] = pv.get("av", 0)
                pv = ev
                ev = {}

                if qc == 2 and h + 1 < H:
                    ld_head[h + 1] = rec_loads(h + 1, head_done_mm[h - 1])

            emit_tail1(15, pv)
            emit_tail2(15, pv)
            head_done_mm[h] = mm.n
        p.fence(CHANS)
        ar.close()

    def a_pass3(o_full):
        ar = ExitStack()
        asb = lambda n, s: ar.enter_context(nc.sbuf_tensor(n, s, F32))
        wo_sb = asb("wo_sb", [128, 8 * D])
        r2_sb = asb("r2_sb", [128, D])
        x_tc = asb("x_tc3", [128, D])
        sq_tc = asb("sq_tc3", [128, D])
        g0_sb = asb("g0_sb", [128, D])
        x1_sb = asb("x1_sb", [128, D])
        srt8 = asb("srt8", [128, 8])
        gm = [asb(f"gm{i}", [128, 1]) for i in range(4)]
        mask8 = asb("mask8", [128, 8])
        sge = asb("sge", [128, 8])
        sgu = asb("sgu", [128, 8])
        gate8 = asb("gate8", [128, 8])
        z_sb = asb("z_sb", [128, 128])
        zT_sb = asb("zT_sb", [128, 128])
        trig_a = asb("trig_a", [128, D])
        trig_b = asb("trig_b", [128, D])
        trig_c = asb("trig_c", [128, D])

        def emit_trig(idx):
            kind, rb = idx // 8, idx % 8
            src0 = theta if kind == 0 else phi
            src1 = thn if kind == 0 else phn
            dst = cwscr if kind == 0 else swscr
            r0 = rb * 128
            p.op("sync", lambda e, src0=src0, r0=r0: e.dma_start(
                out=trig_a[:], in_=src0[r0:r0 + 128, :]),
                wait=[(sc, None)], inc=[(ld, 16)])
            p.op("sync", lambda e, src1=src1, r0=r0: e.dma_start(
                out=trig_b[:], in_=src1[r0:r0 + 128, :]), inc=[(ld, 16)])
            p.op("vector", lambda e: e.tensor_add(trig_a[:], trig_a[:], trig_b[:]),
                 wait=[(ld, None), (sc, None)], inc=[(ve, 1)])
            if kind == 0:
                p.op("scalar", lambda e: e.activation(
                    trig_c[:], trig_a[:], AF.Sin, bias=pih_sb[:], scale=1.0),
                    wait=[(ve, None), (g, None)], inc=[(sc, 1)])
            else:
                p.op("scalar", lambda e: e.activation(
                    trig_c[:], trig_a[:], AF.Sin, bias=0.0, scale=1.0),
                    wait=[(ve, None), (g, None)], inc=[(sc, 1)])
            p.op("gpsimd", lambda e, dst=dst, r0=r0: e.dma_start(
                out=dst[r0:r0 + 128, :], in_=trig_c[:]),
                wait=[(sc, None)], inc=[(g, 16)])

        for j in range(8):
            p.op("sync", lambda e, j=j: e.dma_start(
                out=wo_sb[:, j * D:(j + 1) * D],
                in_=woT[j * 128:(j + 1) * 128, :]), inc=[(ld, 16)])
        p.op("sync", lambda e: e.dma_start(out=r2_sb[:], in_=r2), inc=[(ld, 16)])
        for j in range(8):
            p.op("scalar", lambda e, j=j: e.activation(
                wo_sb[:, j * D:(j + 1) * D], wo_sb[:, j * D:(j + 1) * D], AF.Copy,
                bias=0.0, scale=nw_sb[:, j:j + 1]),
                wait=[(ld, None)], inc=[(sc, 1)])

        def stage_a(tc):
            oslab = o_full[:, tc * D:(tc + 1) * D]
            p.op("scalar", lambda e, oslab=oslab: e.activation(sq_tc[:], oslab, AF.Square),
                 wait=[(ve, None)], inc=[(sc, 1)])

            def rms_v(e):
                e.reduce_sum(gm[0][:], sq_tc[:], axis=AX.X)
                e.drain()
                return e.tensor_scalar(gm[1][:], gm[0][:], 1.0 / D, None, ALU.mult)
            p.op("vector", rms_v, wait=[(sc, None)], inc=[(ve, 1)])
            p.op("scalar", lambda e: e.activation(gm[2][:], gm[1][:], AF.Sqrt,
                                                  bias=eps_rms[:], scale=1.0),
                 wait=[(ve, None)], inc=[(sc, 1)])

            def rms_r(e, oslab=oslab):
                e.reciprocal(gm[3][:], gm[2][:])
                e.drain()
                return e.tensor_scalar(oslab, oslab, gm[3][:], None, ALU.mult)
            p.op("vector", rms_r, wait=[(sc, None)], inc=[(ve, 1)])

            bank0 = BANKS[2 * (tc % 2)]
            bank1 = BANKS[2 * (tc % 2) + 1]

            def otr(e, oslab=oslab, bank0=bank0, bank1=bank1):
                last = None
                for jj in range(8):
                    bank = bank0 if jj < 4 else bank1
                    last = e.transpose(bank[:, (jj % 4) * 128:(jj % 4) * 128 + 128],
                                       oslab[:, jj * 128:(jj + 1) * 128], ident[:])
                return last
            p.op("tensor", otr, wait=[(ve, None), (sc, None)], inc=[(mm, 1)])

            def otrev(e, oslab=oslab, bank0=bank0, bank1=bank1):
                e.activation(oslab[:, 0:512], bank0[:], AF.Copy)
                return e.activation(oslab[:, 512:1024], bank1[:], AF.Copy)
            p.op("scalar", otrev, wait=[(mm, None)], inc=[(sc, 1)])
            return sc.n

        aev = {}

        def stage_b(tc):
            oT = o_full[:, tc * D:(tc + 1) * D]
            for ih in range(2):
                bank = BANKS[4 + 2 * (tc % 2) + ih]

                def wo_mm(e, oT=oT, ih=ih, bank=bank):
                    for ec in range(8):
                        e.matmul(bank[:],
                                 oT[:, ec * 128:(ec + 1) * 128],
                                 wo_sb[:, ec * D + ih * 512: ec * D + (ih + 1) * 512],
                                 start=(ec == 0), stop=False)
                    return e.matmul(bank[:], ones_row[0:1, 0:128],
                                    bo_sb[0:1, ih * 512:(ih + 1) * 512],
                                    start=False, stop=True)
                p.op("tensor", wo_mm, wait=[(sc, aev[tc]), (ve, None)], inc=[(mm, 1)])
            for ih in range(2):
                bank = BANKS[4 + 2 * (tc % 2) + ih]
                p.op("vector", lambda e, oT=oT, ih=ih, bank=bank: e.tensor_scalar(
                    oT[:, ih * 512:(ih + 1) * 512], bank[:], omdw[:], None, ALU.mult),
                    wait=[(mm, None)], inc=[(ve, 1)])

        for tc in range(TC):
            aev[tc] = stage_a(tc)
            if tc > 0:
                stage_b(tc - 1)
        stage_b(TC - 1)

        s0_full = o_full
        for tc in range(TC):
            t0 = tc * 128
            gl = gates_buf[:, tc * 24:tc * 24 + 8]
            ce = gates_buf[:, tc * 24 + 8:tc * 24 + 24]

            def g_v1(e, gl=gl):
                e.max(srt8[:], gl)
                e.drain()
                e.tensor_scalar(gm[0][:], srt8[:, 0:1], -1.0, None, ALU.mult)
                e.drain()
                return e.tensor_single_scalar(mask8[:], gl, srt8[:, 1:2], ALU.is_ge)
            p.op("vector", g_v1, wait=[(sc, None)], inc=[(ve, 1)])
            p.op("scalar", lambda e, gl=gl: e.activation(sge[:], gl, AF.Exp,
                                                         bias=gm[0][:], scale=1.0),
                 wait=[(ve, None)], inc=[(sc, 1)])

            def g_v2(e, ce=ce):
                e.tensor_mul(sgu[:], sge[:], mask8[:])
                e.drain()
                e.reduce_sum(gm[2][:], sgu[:], axis=AX.X)
                e.drain()
                e.reciprocal(gm[3][:], gm[2][:])
                e.drain()
                e.tensor_scalar(gate8[:], sgu[:], gm[3][:], None, ALU.mult)
                e.drain()
                return e.tensor_tensor(
                    z_sb[:].rearrange("p (a b) -> p a b", a=8),
                    gate8[:].unsqueeze(2).to_broadcast([128, 8, 16]),
                    ce.unsqueeze(1).to_broadcast([128, 8, 16]),
                    ALU.mult)
            p.op("vector", g_v2, wait=[(sc, None)], inc=[(ve, 1)])
            p.op("tensor", lambda e: e.transpose(pb2[:, 0:128], z_sb[:], ident[:]),
                 wait=[(ve, None), (sc, None)], inc=[(mm, 1)])
            p.op("scalar", lambda e: e.activation(zT_sb[:], pb2[:, 0:128], AF.Copy),
                 wait=[(mm, None)], inc=[(sc, 1)])
            for oh in range(2):
                bank = BANKS[oh]
                p.op("tensor", lambda e, oh=oh, bank=bank: e.matmul(
                    bank[:], zT_sb[:], r2_sb[:, oh * 512:(oh + 1) * 512],
                    start=True, stop=True),
                    wait=[(sc, None)], inc=[(mm, 1)])
                p.op("scalar", lambda e, oh=oh, bank=bank: e.activation(
                    g0_sb[:, oh * 512:(oh + 1) * 512], bank[:], AF.Silu,
                    bias=0.0, scale=2.0),
                    wait=[(mm, None)], inc=[(sc, 1)])
            p.op("sync", lambda e, t0=t0: e.dma_start(out=x_tc[:], in_=xb[t0:t0 + 128, :]),
                 wait=[(ve, None)], inc=[(ld, 16)])

            def comb(e, tc=tc):
                e.tensor_mul(x1_sb[:], s0_full[:, tc * D:(tc + 1) * D], g0_sb[:])
                return e.tensor_add(x1_sb[:], x1_sb[:], x_tc[:])
            p.op("vector", comb, wait=[(sc, None), (ld, None), (g, None)], inc=[(ve, 1)])
            p.op("gpsimd", lambda e, t0=t0: e.dma_start(
                out=ccin[t0:t0 + 128, :], in_=x1_sb[:]),
                wait=[(ve, None)], inc=[(g, 16)])
            emit_trig(tc)
        p.fence(CHANS)
        ar.close()

    def a2a_and_trig():
        def coll(e):
            with allow_ici_all_to_all():
                return e.collective_compute(
                    "AllToAll", ALU.bypass,
                    replica_groups=[list(range(NCORE))],
                    ins=[ccin.opt()], outs=[ccout.opt()])
        p.op("gpsimd", coll, wait=[(g, None)], inc=[(cc, 1)])
        p.fence(CHANS)

    def phase_b():
        br = ExitStack()
        bsb = lambda n, s: br.enter_context(nc.sbuf_tensor(n, s, F32))
        y_tile = bsb("y_tile", [128, D])
        yre = y_tile
        ynf = bsb("ynf", [128, 8 * 1024])
        vaug = bsb("vaug", [128, 2048])
        prodb = [bsb("prod0", [128, 2048])]
        o1 = bsb("o1", [128, SL * D])
        sq_b = o1
        qkv_ts = bsb("qkv_ts", [128, SL * 1536])
        qf_tmp = bsb("qf_tmp", [128, 1024])
        wbig = bsb("wbig", [128, 4096])
        scob = bsb("scob", [128, 512])
        qe_b = bsb("qe_b", [128, 512])
        mx_b = bsb("mx_b", [128, 64])
        dx_b = bsb("dx_b", [128, 64])
        attb = bsb("attb", [128, 512])
        res_b = bsb("res_b", [128, 512])

        vde_rep = bsb("vde_rep", [128, SL * SL * 64])
        qk_w = int(qkv_ts.shape[1])
        slot_ev = {0: 0, 1: 0}
        for s in range(SL):
            for sp in range(SL):
                dd = min(abs(s - sp), 10)
                p.op("sync", lambda e, s=s, sp=sp, dd=dd: e.dma_start(
                    out=vde_rep[:, s * 512 + sp * 64: s * 512 + (sp + 1) * 64],
                    in_=vde_raw[dd:dd + 1, :].partition_broadcast(128)), inc=[(ld, 16)])

        for tt in range(2):
            tb0 = tt * 128
            for s in range(SL):
                p.op("sync", lambda e, s=s, tb0=tb0: e.dma_start(
                    out=y_tile[:], in_=ccout[s, tb0:tb0 + 128, :]),
                    wait=[(cc, None), (sc, None)], inc=[(ld, 16)])
                emit_ln(y_tile[:], qf_tmp[:], D, sq_b[:, 0:D])

                def tr(e):
                    last = None
                    for jj in range(8):
                        bank = BANKS[jj // 4]
                        last = e.transpose(
                            bank[:, (jj % 4) * 128:(jj % 4) * 128 + 128],
                            qf_tmp[:, jj * 128:(jj + 1) * 128], ident[:])
                    return last
                p.op("tensor", tr, wait=[(sc, None)], inc=[(mm, 1)])

                def trev(e, s=s):
                    last = None
                    for jj in range(8):
                        bank = BANKS[jj // 4]
                        last = e.activation(
                            ynf[:, jj * 1024 + s * 128: jj * 1024 + (s + 1) * 128],
                            bank[:, (jj % 4) * 128:(jj % 4) * 128 + 128], AF.Copy)
                    return last
                p.op("scalar", trev, wait=[(mm, None)], inc=[(sc, 1)])

            ynf_ready = sc.n
            attn_ve_done = ve.n
            av_done = {}
            qeadd_ve = ve.n
            evac_ev = {0: sc.n, 1: sc.n}

            for grp in range(2):
                qtrev_ev = sc.n
                for part in range(3):
                    for icl in range(4):
                        ic = part * 8 + grp * 4 + icl
                        wslot = ((part * 4 + icl) % 2) * 2048
                        bslot = wslot // 2048
                        for j in range(8):
                            p.op("sync", lambda e, ic=ic, j=j, wslot=wslot: e.dma_start(
                                out=wbig[:, wslot + j * 256: wslot + j * 256 + 128],
                                in_=winT[j * 128:(j + 1) * 128, ic * 128:(ic + 1) * 128]),
                                wait=[(mm, slot_ev[wslot // 2048]),
                                      (ve, attn_ve_done)], inc=[(ld, 16)])
                        p.op("sync", lambda e, ic=ic, bslot=bslot: e.dma_start(
                            out=b_in_sb[bslot * 32:bslot * 32 + 1, :],
                            in_=b_in[0:1, ic * 128:(ic + 1) * 128]), inc=[(ld, 16)])
                        dma_ev = ld.n
                        for rh in range(2):
                            bank = BANKS[rh]

                            def qkvmm(e, ic=ic, rh=rh, wslot=wslot, bank=bank):
                                for j in range(8):
                                    e.matmul(
                                        bank[:],
                                        wbig[:, wslot + j * 256: wslot + j * 256 + 128],
                                        ynf[:, j * 1024 + rh * 512: j * 1024 + (rh + 1) * 512],
                                        start=(j == 0), stop=False)
                                return e.matmul(
                                    bank[:],
                                    b_in_sb[(wslot // 2048) * 32:(wslot // 2048) * 32 + 1, :],
                                    ones_row[(wslot // 2048) * 32:(wslot // 2048) * 32 + 1, 0:512],
                                    start=False, stop=True)
                            p.op("tensor", qkvmm,
                                 wait=[(ld, dma_ev), (sc, max(ynf_ready, evac_ev[rh]))],
                                 inc=[(mm, 1)])
                            slot_ev[wslot // 2048] = mm.n
                            p.op("scalar", lambda e, rh=rh, bank=bank, part=part:
                                 e.activation(qf_tmp[:, rh * 512:(rh + 1) * 512],
                                              bank[:], AF.Copy, bias=0.0,
                                              scale=(0.125 if part == 0 else 1.0)),
                                 wait=[(mm, None)], inc=[(sc, 1)])
                            evac_ev[rh] = sc.n

                        if part == 0:
                            def qemm(e):
                                last = None
                                for hl in range(2):
                                    for s in range(SL):
                                        last = e.matmul(
                                            pb2[:, (hl * 8 + s) * 8:(hl * 8 + s) * 8 + 8],
                                            qf_tmp[hl * 64:(hl + 1) * 64,
                                                   s * 128:(s + 1) * 128],
                                            kdeT[hl * 64:(hl + 1) * 64, :],
                                            start=True, stop=True)
                                return last
                            p.op("tensor", qemm, wait=[(sc, evac_ev[1])], inc=[(mm, 1)])
                            p.op("scalar", lambda e, icl=icl: e.activation(
                                qe_b[:, icl * 128:(icl + 1) * 128], pb2[:, 0:128],
                                AF.Copy), wait=[(mm, None), (ve, qeadd_ve)],
                                inc=[(sc, 1)])

                        def qtr(e):
                            last = None
                            for rb in range(8):
                                bank = BANKS[4 + rb // 2]
                                last = e.transpose(
                                    bank[:, (rb % 2) * 128:(rb % 2) * 128 + 128],
                                    qf_tmp[:, rb * 128:(rb + 1) * 128], ident[:])
                            return last
                        p.op("tensor", qtr,
                             wait=[(sc, evac_ev[1]), (ve, attn_ve_done)], inc=[(mm, 1)])

                        def qtrev(e, part=part, icl=icl):
                            last = None
                            for rb in range(8):
                                bank = BANKS[4 + rb // 2]
                                col = rb * 1536 + part * 512 + icl * 128
                                last = e.activation(
                                    qkv_ts[:, col:col + 128],
                                    bank[:, (rb % 2) * 128:(rb % 2) * 128 + 128],
                                    AF.Copy)
                            return last
                        p.op("scalar", qtrev, wait=[(mm, None)], inc=[(sc, 1)])
                        qtrev_ev = sc.n

                for si in range(16):
                    s, hf = si // 2, si % 2
                    prod = prodb[0]

                    def qk(e, s=s, hf=hf, prod=prod):
                        in0 = bass.AP(qkv_ts, s * 1536 + hf * 256,
                                      [[qk_w, 128], [64, 4], [0, 8], [1, 64]])
                        in1 = bass.AP(qkv_ts, 512 + hf * 256,
                                      [[qk_w, 128], [64, 4], [1536, 8], [1, 64]])
                        e.tensor_tensor(
                            bass.AP(prod, 0, [[2048, 128], [512, 4], [64, 8], [1, 64]]),
                            in0, in1, ALU.mult)
                        out = bass.AP(scob, s * 8 + hf * 256,
                                      [[512, 128], [64, 4], [1, 8]])
                        return e.tensor_reduce(
                            out, bass.AP(prod, 0, [[2048, 128], [512, 4], [64, 8], [1, 64]]),
                            AX.X, ALU.add)
                    p.op("vector", qk, wait=[(sc, qtrev_ev)], inc=[(ve, 1)])

                def qe_add(e):
                    for s in range(SL):
                        w = SL - s
                        i0 = bass.AP(scob, s * 8 + s, [[512, 128], [64, 8], [1, w]])
                        i1 = bass.AP(qe_b, s * 8, [[512, 128], [64, 8], [1, w]])
                        e.tensor_tensor(i0, i0, i1, ALU.add)
                        e.drain()
                        if s > 0:
                            i0 = bass.AP(scob, s * 8, [[512, 128], [64, 8], [1, s]])
                            i1 = bass.AP(qe_b, s * 8 + s, [[512, 128], [64, 8], [-1, s]])
                            e.tensor_tensor(i0, i0, i1, ALU.add)
                            e.drain()
                    return e.tensor_reduce(
                        mx_b[:], scob[:].rearrange("p (a b) -> p a b", b=8),
                        AX.X, ALU.max)
                p.op("vector", qe_add, wait=[(sc, None)], inc=[(ve, 1)])
                qeadd_ve = ve.n

                def smx(e):
                    e.drain()
                    return e.tensor_tensor(
                        attb[:].rearrange("p (a b) -> p a b", b=8),
                        scob[:].rearrange("p (a b) -> p a b", b=8),
                        mx_b[:].unsqueeze(2).to_broadcast([128, 64, 8]),
                        ALU.subtract)
                p.op("vector", smx, inc=[(ve, 1)])
                p.op("scalar", lambda e: e.activation(attb[:], attb[:], AF.Exp),
                     wait=[(ve, None)], inc=[(sc, 1)])

                def nrm(e):
                    e.tensor_reduce(dx_b[:], attb[:].rearrange("p (a b) -> p a b", b=8),
                                    AX.X, ALU.add)
                    e.drain()
                    e.reciprocal(dx_b[:], dx_b[:])
                    e.drain()
                    return e.tensor_tensor(
                        attb[:].rearrange("p (a b) -> p a b", b=8),
                        attb[:].rearrange("p (a b) -> p a b", b=8),
                        dx_b[:].unsqueeze(2).to_broadcast([128, 64, 8]),
                        ALU.mult)
                p.op("vector", nrm, wait=[(sc, None)], inc=[(ve, 1)])

                for si in range(16):
                    s, hf = si // 2, si % 2
                    prod = prodb[0]

                    def av(e, s=s, grp=grp, hf=hf, prod=prod):
                        iv = bass.AP(qkv_ts, 1024 + hf * 256,
                                     [[qk_w, 128], [64, 4], [1536, 8], [1, 64]])
                        ivde = bass.AP(vde_rep, s * 512,
                                       [[int(vde_rep.shape[1]), 128], [0, 4], [64, 8], [1, 64]])
                        e.tensor_tensor(
                            bass.AP(vaug, 0, [[2048, 128], [512, 4], [64, 8], [1, 64]]),
                            iv, ivde, ALU.add)
                        ia = bass.AP(attb, s * 8 + hf * 256,
                                     [[512, 128], [64, 4], [0, 64], [1, 8]])
                        ivg = bass.AP(vaug, 0, [[2048, 128], [512, 4], [1, 64], [64, 8]])
                        e.tensor_tensor(
                            bass.AP(prod, 0, [[2048, 128], [512, 4], [8, 64], [1, 8]]),
                            ia, ivg, ALU.mult)
                        out = bass.AP(o1, s * D + grp * 512 + hf * 256,
                                      [[SL * D, 128], [64, 4], [1, 64]])
                        return e.tensor_reduce(
                            out, bass.AP(prod, 0, [[2048, 128], [512, 4], [8, 64], [1, 8]]),
                            AX.X, ALU.add)
                    p.op("vector", av, inc=[(ve, 1)])
                    if hf == 1:
                        av_done[(grp, s)] = ve.n
                attn_ve_done = ve.n

            o1f = qkv_ts
            qkv_mm_done = mm.n
            o1trev_ev = {}
            for s in range(SL):
                def o1tr(e, s=s):
                    last = None
                    for jj in range(8):
                        bank = BANKS[2 + jj // 4]
                        last = e.transpose(
                            bank[:, (jj % 4) * 128:(jj % 4) * 128 + 128],
                            o1[:, s * D + jj * 128: s * D + (jj + 1) * 128], ident[:])
                    return last
                p.op("tensor", o1tr,
                     wait=[(ve, max(av_done[(0, s)], av_done[(1, s)])), (sc, None)],
                     inc=[(mm, 1)])

                def o1trev(e, s=s):
                    last = None
                    for jj in range(8):
                        bank = BANKS[2 + jj // 4]
                        last = e.activation(
                            o1f[:, jj * 1024 + s * 128: jj * 1024 + (s + 1) * 128],
                            bank[:, (jj % 4) * 128:(jj % 4) * 128 + 128], AF.Copy)
                    return last
                p.op("scalar", o1trev, wait=[(mm, None), (ve, attn_ve_done)],
                     inc=[(sc, 1)])
                o1trev_ev[s] = sc.n

            s1 = o1
            for ih in range(2):
                for j in range(8):
                    p.op("sync", lambda e, j=j, ih=ih: e.dma_start(
                        out=wbig[:, j * 512:(j + 1) * 512],
                        in_=woutT[j * 128:(j + 1) * 128, ih * 512:ih * 512 + 512]),
                        wait=[(mm, qkv_mm_done if ih == 0 else mm.n)], inc=[(ld, 16)])
                wo_ld = ld.n
                for rc in range(SL):
                    bank = BANKS[rc % 2]

                    def s1mm(e, rc=rc, ih=ih, bank=bank):
                        for j in range(8):
                            e.matmul(bank[:],
                                     o1f[:, j * 1024 + rc * 128: j * 1024 + (rc + 1) * 128],
                                     wbig[:, j * 512:(j + 1) * 512],
                                     start=(j == 0), stop=False)
                        return e.matmul(bank[:], ones_row[0:1, 0:128],
                                        b_out_sb[0:1, ih * 512:(ih + 1) * 512],
                                        start=False, stop=True)
                    p.op("tensor", s1mm,
                         wait=[(ld, wo_ld), (sc, o1trev_ev[rc] if ih == 0 else sc.n)],
                         inc=[(mm, 1)])
                    p.op("scalar", lambda e, rc=rc, ih=ih, bank=bank: e.activation(
                        s1[:, rc * D + ih * 512: rc * D + (ih + 1) * 512],
                        bank[:], AF.Copy),
                        wait=[(mm, None)], inc=[(sc, 1)])

            sq2_ev = {0: sc.n, 1: sc.n}
            add_ev = {0: ve.n, 1: ve.n}
            gsil_ev = {0: sc.n, 1: sc.n}
            fin_ev = {0: ve.n, 1: ve.n}
            odma_ev = {0: g.n, 1: g.n}
            yld_ev = {0: ld.n, 1: ld.n}
            for oh in range(2):
                for j in range(8):
                    p.op("sync", lambda e, j=j, oh=oh: e.dma_start(
                        out=wbig[:, j * 512:(j + 1) * 512],
                        in_=cwscr[j * 128:(j + 1) * 128, oh * 512:(oh + 1) * 512]),
                        wait=[(mm, None)], inc=[(ld, 16)])
                    p.op("sync", lambda e, j=j, oh=oh: e.dma_start(
                        out=qkv_ts[:, 8192 + j * 512: 8192 + (j + 1) * 512],
                        in_=swscr[j * 128:(j + 1) * 128, oh * 512:(oh + 1) * 512]),
                        wait=[(ve, attn_ve_done)], inc=[(ld, 16)])
                w_ld = ld.n
                for rc in range(SL):
                    par = rc % 2
                    b0, b1 = BANKS[par * 2], BANKS[par * 2 + 1]
                    fr = vaug[:, par * 1024: par * 1024 + 512]
                    fi = vaug[:, par * 1024 + 512: par * 1024 + 1024]
                    yslot = y_tile[:, par * 512:(par + 1) * 512]

                    def ffnmm(e, rc=rc, b0=b0, b1=b1):
                        for j in range(8):
                            e.matmul(b0[:],
                                     ynf[:, j * 1024 + rc * 128: j * 1024 + (rc + 1) * 128],
                                     wbig[:, j * 512:(j + 1) * 512],
                                     start=(j == 0), stop=(j == 7))
                        last = None
                        for j in range(8):
                            last = e.matmul(
                                b1[:],
                                ynf[:, j * 1024 + rc * 128: j * 1024 + (rc + 1) * 128],
                                qkv_ts[:, 8192 + j * 512: 8192 + (j + 1) * 512],
                                start=(j == 0), stop=(j == 7))
                        return last
                    p.op("tensor", ffnmm, wait=[(ld, w_ld), (sc, sq2_ev[par])],
                         inc=[(mm, 1)])
                    mm_ev = mm.n

                    def sq2(e, fr=fr, fi=fi, b0=b0, b1=b1):
                        e.activation(fr, b0[:], AF.Square)
                        return e.activation(fi, b1[:], AF.Square)
                    p.op("scalar", sq2, wait=[(mm, mm_ev), (ve, add_ev[par]),
                                               (g, odma_ev[par])],
                         inc=[(sc, 1)])
                    sq2_ev[par] = sc.n
                    p.op("vector", lambda e, fr=fr, fi=fi: e.tensor_add(fr, fr, fi),
                         wait=[(sc, sq2_ev[par]), (ve, fin_ev[par])], inc=[(ve, 1)])
                    add_ev[par] = ve.n

                    def gsil(e, fr=fr):
                        e.activation(fr, fr, AF.Sqrt)
                        return e.activation(fr, fr, AF.Silu)
                    p.op("scalar", gsil, wait=[(ve, add_ev[par])], inc=[(sc, 1)])
                    gsil_ev[par] = sc.n
                    p.op("sync", lambda e, rc=rc, oh=oh, tb0=tb0, yslot=yslot: e.dma_start(
                        out=yslot, in_=ccout[rc, tb0:tb0 + 128, oh * 512:(oh + 1) * 512]),
                        wait=[(ve, fin_ev[par])], inc=[(ld, 16)])
                    yld_ev[par] = ld.n

                    def fin(e, rc=rc, oh=oh, fr=fr, yslot=yslot):
                        e.tensor_mul(fr, fr,
                                     s1[:, rc * D + oh * 512: rc * D + (oh + 1) * 512])
                        return e.tensor_add(fr, fr, yslot)
                    p.op("vector", fin, wait=[(sc, gsil_ev[par]), (g, odma_ev[par]),
                                              (ld, yld_ev[par])], inc=[(ve, 1)])
                    fin_ev[par] = ve.n
                    p.op("gpsimd", lambda e, rc=rc, oh=oh, tb0=tb0, fr=fr: e.dma_start(
                        out=out_ext[rc, tb0:tb0 + 128, oh * 512:(oh + 1) * 512],
                        in_=fr),
                        wait=[(ve, fin_ev[par])], inc=[(g, 16)])
                    odma_ev[par] = g.n
            p.fence(CHANS)
        br.close()

    a_pass1()
    ofs = ExitStack()
    o_full = ofs.enter_context(nc.sbuf_tensor("o_full", [128, TC * D], F32))
    a_attn(o_full)
    a_pass3(o_full)
    ofs.close()
    a2a_and_trig()
    phase_b()
    p.fence(CHANS)
    p.emit()
    es.close()
    return nc


_NC_CACHE = None


def _get_nc():
    global _NC_CACHE
    if _NC_CACHE is None:
        _NC_CACHE = build_program()
    return _NC_CACHE


def kernel(L0_attn_x, L0_attn_Wq, L0_attn_Wk, L0_attn_Wv, L0_attn_norm_w,
           L0_attn_Wo, L0_attn_bo, L0_attn_diff_weight, L0_ffn_w_route,
           L0_ffn_compress, L0_ffn_routed, L1_attn_W_in, L1_attn_b_in,
           L1_attn_k_dist_emb, L1_attn_v_dist_emb, L1_attn_W_out, L1_attn_b_out,
           L1_ffn_theta, L1_ffn_phi, L1_ffn_theta_noise, L1_ffn_phi_noise):
    f = np.float32
    c = np.ascontiguousarray
    x = c(np.asarray(L0_attn_x, f))
    shared = {
        "wqT": c(np.asarray(L0_attn_Wq, f).T),
        "wkT": c(np.asarray(L0_attn_Wk, f).T),
        "wvT": c(np.asarray(L0_attn_Wv, f).T),
        "woT": c(np.asarray(L0_attn_Wo, f).T),
        "nw_col": c(np.asarray(L0_attn_norm_w, f).reshape(8, 128).T),
        "bo": c(np.asarray(L0_attn_bo, f).reshape(1, D)),
        "dw_in": c(np.asarray(L0_attn_diff_weight, f).reshape(1, 1)),
        "wg": c(np.concatenate([np.asarray(L0_ffn_w_route, f).T,
                                np.asarray(L0_ffn_compress, f).T], axis=1)),
        "r2": c(np.asarray(L0_ffn_routed, f).transpose(0, 2, 1).reshape(128, D)),
        "winT": c(np.asarray(L1_attn_W_in, f).T),
        "b_in": c(np.asarray(L1_attn_b_in, f).reshape(1, 3 * D)),
        "kde_raw": c(np.asarray(L1_attn_k_dist_emb, f)),
        "vde_raw": c(np.asarray(L1_attn_v_dist_emb, f)),
        "woutT": c(np.asarray(L1_attn_W_out, f).T),
        "b_out": c(np.asarray(L1_attn_b_out, f).reshape(1, D)),
        "theta": c(np.asarray(L1_ffn_theta, f)),
        "thn": c(np.asarray(L1_ffn_theta_noise, f)),
        "phi": c(np.asarray(L1_ffn_phi, f)),
        "phn": c(np.asarray(L1_ffn_phi_noise, f)),
        "ident": np.eye(128, dtype=f),
    }
    in_maps = [dict(shared, xb=c(x[i])) for i in range(NCORE)]
    nc = _get_nc()
    res = run_bass_kernel_spmd(nc, in_maps, core_ids=list(range(NCORE)))
    return np.concatenate([res.results[i]["out"] for i in range(NCORE)], axis=1)
